# version 20
# baseline (speedup 1.0000x reference)
import math
from contextlib import ExitStack

import numpy as np
import ml_dtypes

import concourse.bass as bass
import concourse.mybir as mybir
from concourse.bass_utils import run_bass_kernel_spmd

F32 = mybir.dt.float32
BF16 = mybir.dt.bfloat16
AF = mybir.ActivationFunctionType
OP = mybir.AluOpType
AX = mybir.AxisListType

S, D, NH, HD, NKV = 8192, 2048, 16, 128, 4
DHID = 8192
NC = 8
NLT = 8
ALPHA = 1.0 / math.sqrt(HD)
NEGBIG = -3.0e37

ENGS = ("sync", "vector", "scalar", "tensor", "gpsimd")


class Sched:

    def __init__(self):
        self.ops = {e: [] for e in ENGS}
        self.cnt = {}

    def emit(self, eng, fn, waits=(), inc=None, n=1):
        waits = [w for w in waits if w is not None]
        self.ops[eng].append((fn, waits, inc, n))
        if inc is None:
            return None
        self.cnt[inc] = self.cnt.get(inc, 0) + n
        return (inc, self.cnt[inc])

    def last(self, sem):
        return (sem, self.cnt.get(sem, 0))

    def barrier(self):
        toks = [self.last(s) for s in list(self.cnt.keys())]
        for eng in ENGS:
            self.emit(eng, lambda x: x.nop(), waits=toks, inc={"sync": None}.get(eng))

    def replay(self, name, eng, sems):
        for fn, waits, inc, n in self.ops[name]:
            for s, v in waits:
                eng.wait_ge(sems[s], v)
            ins = fn(eng)
            if inc is not None:
                ins.then_inc(sems[inc], n)


class Pool:

    def __init__(self, k):
        self.k = k
        self.free = [None] * k
        self.i = 0

    def get(self):
        j = self.i % self.k
        self.i += 1
        return j, self.free[j]

    def put(self, j, tok):
        self.free[j] = tok


def build_nc():
    nc = bass.Bass("TRN2", target_bir_lowering=False, debug=False)
    dt = nc.dram_tensor
    xfull = dt("xfull", [S, D], F32, kind="ExternalInput").ap()
    xown = dt("xown", [NLT, 128, D], F32, kind="ExternalInput").ap()
    wq = dt("wq", [NH, 128, 2048], F32, kind="ExternalInput").ap()
    wk = dt("wk", [128, 16 * 512], F32, kind="ExternalInput").ap()
    wv = dt("wv", [128, 16 * 512], F32, kind="ExternalInput").ap()
    wo = dt("wo", [16, 128, 2048], F32, kind="ExternalInput").ap()
    wup = dt("wup", [4, 16, 128, 2048], F32, kind="ExternalInput").ap()
    wd = dt("wd", [4, 16, 128, 2048], F32, kind="ExternalInput").ap()
    bqc = dt("bqc", [128, 16], F32, kind="ExternalInput").ap()
    bkc = dt("bkc", [128, 4], F32, kind="ExternalInput").ap()
    bvc = dt("bvc", [128, 4], F32, kind="ExternalInput").ap()
    boc = dt("boc", [128, 16], F32, kind="ExternalInput").ap()
    upsc = dt("upsc", [128, 64], F32, kind="ExternalInput").ap()
    dsc = dt("dsc", [128, 16], F32, kind="ExternalInput").ap()
    osc = dt("osc", [128, 16], F32, kind="ExternalInput").ap()
    cosk = dt("cosk", [128, S], F32, kind="ExternalInput").ap()
    sink = dt("sink", [128, S], F32, kind="ExternalInput").ap()
    cosq = dt("cosq", [128, 1024], F32, kind="ExternalInput").ap()
    sinq = dt("sinq", [128, 1024], F32, kind="ExternalInput").ap()
    mska = dt("mska", [128, 512], F32, kind="ExternalInput").ap()
    mskb = dt("mskb", [128, 512], F32, kind="ExternalInput").ap()
    idb = dt("idb", [128, 128], BF16, kind="ExternalInput").ap()
    idf = dt("idf", [128, 128], F32, kind="ExternalInput").ap()
    kdr = dt("kdr", [NKV, 128, S], BF16, kind="Internal").ap()
    vdr = dt("vdr", [NKV, 128, S], BF16, kind="Internal").ap()
    fdr = dt("fdr", [128, 16 * 1024], F32, kind="Internal").ap()
    oout = dt("oout", [NLT, 128, D], F32, kind="ExternalOutput").ap()

    sd = Sched()
    SWAP32 = [(i + 1 if i % 2 == 0 else i - 1) for i in range(32)]

    with ExitStack() as top:
        e = top.enter_context
        identb = e(nc.sbuf_tensor("identb", [128, 128], BF16))
        identf = e(nc.sbuf_tensor("identf", [128, 128], F32))
        bq_s = e(nc.sbuf_tensor("bq_s", [128, 16], F32))
        bk_s = e(nc.sbuf_tensor("bk_s", [128, 4], F32))
        bv_s = e(nc.sbuf_tensor("bv_s", [128, 4], F32))
        bo_s = e(nc.sbuf_tensor("bo_s", [128, 16], F32))
        upsc_s = e(nc.sbuf_tensor("upsc_s", [128, 64], F32))
        dsc_s = e(nc.sbuf_tensor("dsc_s", [128, 16], F32))
        osc_s = e(nc.sbuf_tensor("osc_s", [128, 16], F32))
        dsos_s = e(nc.sbuf_tensor("dsos_s", [128, 16], F32))
        xnown = e(nc.sbuf_tensor("xnown", [128, 16 * 1024], BF16))

        for dst, src in ((identb, idb), (identf, idf), (bq_s, bqc), (bk_s, bkc),
                         (bv_s, bvc), (bo_s, boc), (upsc_s, upsc), (dsc_s, dsc),
                         (osc_s, osc)):
            sd.emit("sync", (lambda d, s: lambda g: g.dma_start(out=d[:], in_=s))(dst, src),
                    inc="dM", n=16)
        tok_consts = sd.last("dM")
        tok_dsos = sd.emit("vector", lambda v: v.tensor_tensor(
            out=dsos_s[:], in0=dsc_s[:], in1=osc_s[:], op=OP.mult), waits=[tok_consts], inc="vv")
        sd.emit("tensor", lambda t: t.nop(), waits=[tok_consts])
        sd.emit("scalar", lambda s_: s_.nop(), waits=[tok_consts])

        tok_fsp_last = [None]
        with ExitStack() as s1:
            e1 = s1.enter_context
            coss = [e1(nc.sbuf_tensor(f"coss{i}", [128, 512], F32)) for i in range(2)]
            sins = [e1(nc.sbuf_tensor(f"sins{i}", [128, 512], F32)) for i in range(2)]
            wkt = e1(nc.sbuf_tensor("wkt", [128, 16 * 512], BF16))
            wvt = e1(nc.sbuf_tensor("wvt", [128, 16 * 512], BF16))
            xf = [e1(nc.sbuf_tensor(f"xf{i}", [128, D], F32)) for i in range(2)]
            xnt = [e1(nc.sbuf_tensor(f"xnt{i}", [128, D], BF16)) for i in range(2)]
            xnf = e1(nc.sbuf_tensor("xnf", [128, 16 * 512], BF16))
            bn6 = [e1(nc.sbuf_tensor(f"bn6{i}", [128, 24], F32)) for i in range(2)]
            bn2 = [e1(nc.sbuf_tensor(f"bn2{i}", [128, 2], F32)) for i in range(2)]
            varp = [e1(nc.sbuf_tensor(f"varp{i}", [128, 1], F32)) for i in range(2)]
            rvar = [e1(nc.sbuf_tensor(f"rvar{i}", [128, 1], F32)) for i in range(2)]
            rstd = [e1(nc.sbuf_tensor(f"rstd{i}", [128, 1], F32)) for i in range(2)]
            nmr = [e1(nc.sbuf_tensor(f"nmr{i}", [128, 1], F32)) for i in range(2)]
            kraw = e1(nc.sbuf_tensor("kraw", [128, 512], F32))
            kswp = e1(nc.sbuf_tensor("kswp", [128, 512], F32))
            kc1 = e1(nc.sbuf_tensor("kc1", [128, 512], F32))
            kc2 = e1(nc.sbuf_tensor("kc2", [128, 512], F32))
            kbf = [e1(nc.sbuf_tensor(f"kbf{i}", [128, 512], BF16)) for i in range(2)]
            vb16 = [e1(nc.sbuf_tensor(f"vb16{i}", [128, 512], BF16)) for i in range(2)]
            vkm = [e1(nc.sbuf_tensor(f"vkm{i}", [128, 512], BF16)) for i in range(2)]
            ps_xn = [e1(nc.psum_tensor(f"ps_xn{i}", [128, 1024], BF16)) for i in range(2)]
            ps_kv = [e1(nc.psum_tensor(f"ps_kv{i}", [128, 512], F32)) for i in range(2)]
            ps_vt = e1(nc.psum_tensor("ps_vt", [128, 512], BF16))
            wub = [e1(nc.sbuf_tensor(f"wub{i}", [128, 2048], BF16)) for i in range(2)]
            wdf = [e1(nc.sbuf_tensor(f"wdf{i}", [128, 2048], F32)) for i in range(2)]
            wdb = [e1(nc.sbuf_tensor(f"wdb{i}", [128, 2048], BF16)) for i in range(2)]
            hjb = e1(nc.sbuf_tensor("hjb", [128, 16 * 1024], BF16))
            rbk = [e1(nc.sbuf_tensor(f"rbk{i}", [128, 512], F32)) for i in range(4)]
            obuf = [e1(nc.sbuf_tensor(f"obuf{i}", [128, 512], F32)) for i in range(2)]
            psF = [e1(nc.psum_tensor(f"psF{i}", [128, 512], F32)) for i in range(2)]

            tok_wkt = sd.emit("gpsimd", lambda g: g.dma_start(out=wkt[:], in_=wk), inc="dG", n=16)
            tok_wvt = sd.emit("gpsimd", lambda g: g.dma_start(out=wvt[:], in_=wv), inc="dG", n=16)

            xf_free = [None, None]
            xnt_free = [None, None]
            lnb_free = [None, None]
            psxn = Pool(2)

            def ln_tile(i, src, dst_fn, evac_waits=()):
                b = i % 2
                tx = sd.emit("scalar", lambda g: g.dma_start(out=xf[b][:], in_=src),
                             waits=[xf_free[b]], inc="dX", n=16)
                for j in range(4):
                    ins = sd.emit("vector", (lambda b, j: lambda v: v.bn_stats(
                        out=bn6[b][:, 6 * j:6 * j + 6], in_=xf[b][:, 512 * j:512 * (j + 1)]))(b, j),
                        waits=[tx, lnb_free[b]] if j == 0 else [], inc="vv" if j == 3 else None)
                sd.emit("vector", (lambda b: lambda v: v.bn_aggr(out=bn2[b][:], in_=bn6[b][:]))(b),
                        waits=[ins], inc="vv")
                g1 = sd.last("vv")
                sd.emit("vector", (lambda b: lambda v: v.tensor_scalar_add(
                    out=varp[b][:], in0=bn2[b][:, 1:2], scalar1=1e-5))(b), waits=[g1], inc="vv")
                g2 = sd.last("vv")
                tv = sd.emit("vector", (lambda b: lambda v: v.reciprocal(
                    out=rvar[b][:], in_=varp[b][:]))(b), waits=[g2], inc="vv")
                ts_ = sd.emit("scalar", (lambda b: lambda s: s.activation(
                    out=rstd[b][:], in_=rvar[b][:], func=AF.Sqrt))(b), waits=[tv], inc="ss")
                sd.emit("vector", (lambda b: lambda v: v.tensor_tensor(
                    out=nmr[b][:], in0=bn2[b][:, 0:1], in1=rstd[b][:], op=OP.mult))(b),
                    waits=[ts_], inc="vv")
                g3 = sd.last("vv")
                tn = sd.emit("vector", (lambda b: lambda v: v.tensor_scalar_mul(
                    nmr[b][:], nmr[b][:], -1.0))(b), waits=[g3], inc="vv")
                lnb_free[b] = tn
                tnorm = sd.emit("scalar", (lambda b: lambda s: s.activation(
                    out=xnt[b][:], in_=xf[b][:], func=AF.Identity,
                    bias=nmr[b][:], scale=rstd[b][:]))(b),
                    waits=[tn, xnt_free[b]], inc="ss")
                xf_free[b] = tnorm
                tev = None
                for r in range(2):
                    sl, sfree = psxn.get()
                    for j in range(8):
                        tt_ = sd.emit("tensor", (lambda b, sl, r, j: lambda t: t.transpose(
                            ps_xn[sl][:, 128 * j:128 * (j + 1)],
                            xnt[b][:, 128 * (8 * r + j):128 * (8 * r + j + 1)],
                            identb[:]))(b, sl, r, j),
                            waits=[tnorm, sfree] if j == 0 else [],
                            inc="tt" if j == 7 else None)
                    if r == 1:
                        xnt_free[b] = tt_
                    dst = dst_fn(r)
                    tev = sd.emit("scalar", (lambda sl, dst: lambda s: s.copy(
                        out=dst, in_=ps_xn[sl].ap().rearrange("p (a n) -> p a n", n=128)))(sl, dst),
                        waits=[tt_] + (list(evac_waits) if r == 0 else []), inc="ss")
                    psxn.put(sl, tev)
                return tev

            xnown3 = xnown.ap().rearrange("p (a n) -> p a n", n=1024)
            tok_xnown = None
            for l in range(NLT):
                dst_fn = (lambda l: lambda r: xnown3[:, 8 * r:8 * r + 8, 128 * l:128 * (l + 1)])(l)
                tok_xnown = ln_tile(l, xown[l, :, :], dst_fn)

            xnf_free = [None]
            last_krope = [None]
            cos_free = [None, None]
            pskv = Pool(2)
            spill_kbf = Pool(2)
            spill_v = Pool(2)
            psvt_free = [None]

            xnf3 = xnf.ap().rearrange("p (a n) -> p a n", n=512)

            def gen_p0():
                for g in range(16):
                    gb = g % 2
                    tca = sd.emit("sync", (lambda g, gb: lambda q: q.dma_start(
                        out=coss[gb][:], in_=cosk[:, 512 * g:512 * (g + 1)]))(g, gb),
                        waits=[cos_free[gb]], inc="dT", n=16)
                    tcb = sd.emit("sync", (lambda g, gb: lambda q: q.dma_start(
                        out=sins[gb][:], in_=sink[:, 512 * g:512 * (g + 1)]))(g, gb), inc="dT", n=16)
                    evs = []
                    for tt4 in range(4):
                        i = 4 * g + tt4
                        dst_fn = (lambda tt4: lambda r: xnf3[:, 8 * r:8 * r + 8,
                                                             128 * tt4:128 * (tt4 + 1)])(tt4)
                        ew = [xnf_free[0]] if tt4 == 0 else []
                        evs.append(ln_tile(8 + i, xfull[128 * i:128 * (i + 1), :], dst_fn,
                                           evac_waits=ew))
                        yield
                    tok_grp = evs[-1]
                    for kvc in range(NKV):
                        pj, pfree = pskv.get()
                        for dc in range(16):
                            w0 = [tok_grp, tok_wkt, pfree] if dc == 0 else []
                            mm = sd.emit("tensor", (lambda pj, dc, kvc: lambda t: t.matmul(
                                ps_kv[pj][:], wkt[:, 512 * dc + 128 * kvc:512 * dc + 128 * (kvc + 1)],
                                xnf[:, 512 * dc:512 * (dc + 1)],
                                start=(dc == 0), stop=(dc == 15)))(pj, dc, kvc),
                                waits=w0, inc="tt" if dc == 15 else None)
                        tev = sd.emit("vector", (lambda pj, kvc: lambda v: v.tensor_scalar_add(
                            out=kraw[:], in0=ps_kv[pj][:], scalar1=bk_s[:, kvc:kvc + 1]))(pj, kvc),
                            waits=[mm, last_krope[0]], inc="vv")
                        pskv.put(pj, tev)
                        tshf = sd.emit("vector", lambda v: v.stream_shuffle(
                            out=kswp[:], in_=kraw[:], mask=SWAP32), inc="vv")
                        sd.emit("gpsimd", (lambda gb: lambda v: v.tensor_tensor(
                            out=kc1[:], in0=kswp[:], in1=sins[gb][:], op=OP.mult))(gb),
                            waits=[tcb, tshf])
                        tkc2 = sd.emit("gpsimd", (lambda gb: lambda v: v.tensor_tensor(
                            out=kc2[:], in0=kraw[:], in1=coss[gb][:], op=OP.mult))(gb),
                            waits=[tca], inc="gg" if kvc == NKV - 1 else None)
                        if kvc == NKV - 1:
                            cos_free[gb] = tkc2
                        sb, sfree = spill_kbf.get()
                        tkb = sd.emit("gpsimd", (lambda sb: lambda v: v.tensor_tensor(
                            out=kbf[sb][:], in0=kc1[:], in1=kc2[:], op=OP.add))(sb),
                            waits=[sfree], inc="gg")
                        last_krope[0] = tkb
                        tsp = sd.emit("sync", (lambda sb, kvc, g: lambda q: q.dma_start(
                            out=kdr[kvc, :, 512 * g:512 * (g + 1)], in_=kbf[sb][:]))(sb, kvc, g),
                            waits=[tkb], inc="dS", n=16)
                        spill_kbf.put(sb, tsp)
                        pj, pfree = pskv.get()
                        for dc in range(16):
                            w0 = [pfree] if dc == 0 else []
                            mm = sd.emit("tensor", (lambda pj, dc, kvc: lambda t: t.matmul(
                                ps_kv[pj][:], wvt[:, 512 * dc + 128 * kvc:512 * dc + 128 * (kvc + 1)],
                                xnf[:, 512 * dc:512 * (dc + 1)],
                                start=(dc == 0), stop=(dc == 15)))(pj, dc, kvc),
                                waits=w0 + ([tok_wvt] if (g == 0 and kvc == 0 and dc == 0) else []),
                                inc="tt" if dc == 15 else None)
                        vbj, vbfree = spill_v.get()
                        tvb = sd.emit("vector", (lambda pj, kvc, vbj: lambda v: v.tensor_scalar_add(
                            out=vb16[vbj][:], in0=ps_kv[pj][:], scalar1=bv_s[:, kvc:kvc + 1]))(pj, kvc, vbj),
                            waits=[mm, vbfree], inc="vv")
                        pskv.put(pj, tvb)
                        for bb in range(4):
                            ttr = sd.emit("tensor", (lambda vbj, bb: lambda t: t.transpose(
                                ps_vt[:, 128 * bb:128 * (bb + 1)], vb16[vbj][:, 128 * bb:128 * (bb + 1)],
                                identb[:]))(vbj, bb),
                                waits=[tvb, psvt_free[0]] if bb == 0 else [],
                                inc="tt" if bb == 3 else None)
                        tvk = sd.emit("vector", (lambda vbj: lambda v: v.tensor_copy(
                            out=vkm[vbj][:], in_=ps_vt[:, 0:512]))(vbj), waits=[ttr], inc="vv")
                        psvt_free[0] = tvk
                        tsp = sd.emit("sync", (lambda vbj, kvc, g: lambda q: q.dma_start(
                            out=vdr[kvc, :, 512 * g:512 * (g + 1)], in_=vkm[vbj][:]))(vbj, kvc, g),
                            waits=[tvk], inc="dS", n=16)
                        spill_v.put(vbj, tsp)
                        yield
                    xnf_free[0] = mm

            def gen_ffn():
                psf = Pool(2)
                wub_free = [None, None]
                wdf_free = [None, None]
                wdb_free = [None, None]
                rbkp = Pool(4)
                obp = Pool(2)
                spill_tok = [[None, None] for _ in range(16)]
                rb_pend = [[None, None] for _ in range(16)]
                rb_slot = [[None, None] for _ in range(16)]
                hjb_used = None

                def prefetch_rb(oc):
                    for half in range(2):
                        rj, rfree = rbkp.get()
                        trb = sd.emit("scalar", (lambda rj, oc, half: lambda v: v.dma_start(
                            out=rbk[rj][:],
                            in_=fdr[:, 1024 * oc + 512 * half:1024 * oc + 512 * (half + 1)]))(rj, oc, half),
                            waits=[spill_tok[oc][half], rfree], inc="dR", n=16)
                        rb_pend[oc][half] = trb
                        rb_slot[oc][half] = rj

                for jb in range(4):
                    up_last = None
                    for jc in range(16):
                        cb = jc % 2
                        twf = sd.emit("gpsimd", (lambda jb, jc, cb: lambda g: g.dma_start(
                            out=wub[cb][:], in_=wup[jb, jc, :, :]))(jb, jc, cb),
                            waits=[wub_free[cb]], inc="dG", n=16)
                        tsg = sd.emit("scalar", (lambda cb: lambda s: s.sign(
                            out=wub[cb][:], in_=wub[cb][:]))(cb),
                            waits=[twf], inc="ss")
                        for half in range(2):
                            pj, pfree = psf.get()
                            for dc in range(16):
                                w0 = ([tsg, pfree, tok_xnown] +
                                      ([hjb_used] if (jc == 0 and half == 0) else [])) if dc == 0 else []
                                mm = sd.emit("tensor", (lambda pj, cb, dc, half: lambda t: t.matmul(
                                    psF[pj][:], wub[cb][:, 128 * dc:128 * (dc + 1)],
                                    xnown[:, 1024 * dc + 512 * half:1024 * dc + 512 * (half + 1)],
                                    start=(dc == 0), stop=(dc == 15)))(pj, cb, dc, half),
                                    waits=w0, inc="tt" if dc == 15 else None)
                            tev = sd.emit("vector", (lambda pj, jb, jc, half: lambda v: v.tensor_scalar(
                                out=hjb[:, 1024 * jc + 512 * half:1024 * jc + 512 * (half + 1)],
                                in0=psF[pj][:], scalar1=upsc_s[:, 16 * jb + jc:16 * jb + jc + 1],
                                scalar2=0.0, op0=OP.mult, op1=OP.max))(pj, jb, jc, half),
                                waits=[mm], inc="vv")
                            psf.put(pj, tev)
                            up_last = tev
                        wub_free[cb] = mm
                        yield
                    if jb > 0:
                        prefetch_rb(0)
                        prefetch_rb(1)
                    for oc in range(16):
                        cb = oc % 2
                        twf = sd.emit("sync", (lambda jb, oc, cb: lambda q: q.dma_start(
                            out=wdf[cb][:], in_=wd[jb, oc, :, :]))(jb, oc, cb),
                            waits=[wdf_free[cb]], inc="dD", n=16)
                        tsg = sd.emit("scalar", (lambda cb: lambda s: s.sign(
                            out=wdb[cb][:], in_=wdf[cb][:]))(cb),
                            waits=[twf, wdb_free[cb]], inc="ss")
                        wdf_free[cb] = tsg
                        for half in range(2):
                            pj, pfree = psf.get()
                            for jc in range(16):
                                w0 = [tsg, up_last, pfree] if jc == 0 else []
                                mm = sd.emit("tensor", (lambda pj, cb, jc, half: lambda t: t.matmul(
                                    psF[pj][:], wdb[cb][:, 128 * jc:128 * (jc + 1)],
                                    hjb[:, 1024 * jc + 512 * half:1024 * jc + 512 * (half + 1)],
                                    start=(jc == 0), stop=(jc == 15)))(pj, cb, jc, half),
                                    waits=w0, inc="tt" if jc == 15 else None)
                            oj, ofree = obp.get()
                            if jb == 0:
                                tev = sd.emit("vector", (lambda pj, oj, oc: lambda v: v.tensor_single_scalar(
                                    out=obuf[oj][:], in_=psF[pj][:],
                                    scalar=dsos_s[:, oc:oc + 1], op=OP.mult))(pj, oj, oc),
                                    waits=[mm, tok_dsos, ofree], inc="vv")
                            else:
                                rj = rb_slot[oc][half]
                                tev = sd.emit("vector", (lambda pj, oj, rj, oc: lambda v: v.scalar_tensor_tensor(
                                    out=obuf[oj][:], in0=psF[pj][:],
                                    scalar=dsos_s[:, oc:oc + 1], in1=rbk[rj][:],
                                    op0=OP.mult, op1=OP.add))(pj, oj, rj, oc),
                                    waits=[mm, rb_pend[oc][half], ofree], inc="vv")
                                rbkp.put(rj, tev)
                            psf.put(pj, tev)
                            tsp = sd.emit("sync", (lambda oj, oc, half: lambda v: v.dma_start(
                                out=fdr[:, 1024 * oc + 512 * half:1024 * oc + 512 * (half + 1)],
                                in_=obuf[oj][:]))(oj, oc, half),
                                waits=[tev], inc="dF", n=16)
                            obp.put(oj, tsp)
                            spill_tok[oc][half] = tsp
                        if jb > 0 and oc + 2 < 16:
                            prefetch_rb(oc + 2)
                        wdb_free[cb] = mm
                        yield
                    hjb_used = mm
                tok_fsp_last[0] = sd.last("dF")

            gA, gB = gen_p0(), gen_ffn()
            aA = aB = True
            while aA or aB:
                if aA:
                    try:
                        next(gA)
                    except StopIteration:
                        aA = False
                if aB:
                    try:
                        next(gB)
                    except StopIteration:
                        aB = False

        sd.barrier()
        tok_spill_all = sd.last("dS")
        sA = ExitStack()
        attfm = sA.enter_context(nc.sbuf_tensor("attfm", [128, 16 * 1024], BF16))
        with ExitStack() as s2:
            e2 = s2.enter_context
            cosq_s = e2(nc.sbuf_tensor("cosq_s", [128, 1024], F32))
            sinq_s = e2(nc.sbuf_tensor("sinq_s", [128, 1024], F32))
            mska_s = e2(nc.sbuf_tensor("mska_s", [128, 512], F32))
            mskb_s = e2(nc.sbuf_tensor("mskb_s", [128, 512], F32))
            kg = [e2(nc.sbuf_tensor(f"kg{i}", [128, S], BF16)) for i in range(2)]
            vg = [e2(nc.sbuf_tensor(f"vg{i}", [128, S], BF16)) for i in range(2)]
            wqt = [e2(nc.sbuf_tensor(f"wqt{i}", [128, 2048], BF16)) for i in range(2)]
            qraw = e2(nc.sbuf_tensor("qraw", [128, 1024], F32))
            qswp = e2(nc.sbuf_tensor("qswp", [128, 1024], F32))
            qc1 = e2(nc.sbuf_tensor("qc1", [128, 1024], F32))
            qc2 = e2(nc.sbuf_tensor("qc2", [128, 1024], F32))
            qb = [e2(nc.sbuf_tensor(f"qb{i}", [128, 1024], BF16)) for i in range(2)]
            pwp = [e2(nc.sbuf_tensor(f"pwp{i}", [128, 1024], BF16)) for i in range(4)]
            ptw = [e2(nc.sbuf_tensor(f"ptw{i}", [128, 1024], BF16)) for i in range(3)]
            nmc = e2(nc.sbuf_tensor("nmc", [128, 32], F32))
            denc = e2(nc.sbuf_tensor("denc", [128, 32], F32))
            nmmin = e2(nc.sbuf_tensor("nmmin", [128, 2], F32))
            d8 = e2(nc.sbuf_tensor("d8", [128, 32], F32))
            u8 = e2(nc.sbuf_tensor("u8", [128, 32], F32))
            u8b = e2(nc.sbuf_tensor("u8b", [128, 32], F32))
            t8 = e2(nc.sbuf_tensor("t8", [128, 32], F32))
            denf = e2(nc.sbuf_tensor("denf", [128, 2], F32))
            rden = e2(nc.sbuf_tensor("rden", [128, 2], F32))
            accw = [e2(nc.sbuf_tensor(f"accw{i}", [128, 1024], F32)) for i in range(2)]
            accf = [e2(nc.sbuf_tensor(f"accf{i}", [128, 128], F32)) for i in range(2)]
            attt = [e2(nc.sbuf_tensor(f"attt{i}", [128, 128], BF16)) for i in range(2)]
            psqk = [e2(nc.psum_tensor(f"psqk{i}", [128, 1024], F32)) for i in range(2)]
            pstr = [e2(nc.psum_tensor(f"pstr{i}", [128, 1024], BF16)) for i in range(2)]
            pspv = [e2(nc.psum_tensor(f"pspv{i}", [128, 512], F32)) for i in range(2)]

            sd.emit("sync", lambda q: q.dma_start(out=cosq_s[:], in_=cosq), inc="dT", n=16)
            tok_qt = sd.emit("sync", lambda q: q.dma_start(out=sinq_s[:], in_=sinq), inc="dT", n=16)
            sd.emit("sync", lambda q: q.dma_start(out=mska_s[:], in_=mska), inc="dT", n=16)
            tok_masks = sd.emit("sync", lambda q: q.dma_start(out=mskb_s[:], in_=mskb), inc="dT", n=16)

            qkp = Pool(2)
            pwq = Pool(4)
            trp = Pool(2)
            ptwp = Pool(3)
            pvp = Pool(2)
            kg_free = [None, None]
            vg_free = [None, None]
            wqt_free = [None, None]
            last_qrope = [None]
            tok_attfm = None

            for g in range(NKV):
                gb = g % 2
                tkg = sd.emit("sync", (lambda g, gb: lambda q: q.dma_start(
                    out=kg[gb][:], in_=kdr[g, :, :]))(g, gb),
                    waits=[tok_spill_all, kg_free[gb]], inc="dK", n=16)
                tvg = sd.emit("sync", (lambda g, gb: lambda q: q.dma_start(
                    out=vg[gb][:], in_=vdr[g, :, :]))(g, gb),
                    waits=[vg_free[gb]], inc="dK", n=16)
                for pair2 in range(2):
                    heads = [4 * g + 2 * pair2, 4 * g + 2 * pair2 + 1]
                    tokqs = [None, None]
                    acc_toks_all = [[], []]
                    merge_tok = [None, None]
                    for ln_, h in enumerate(heads):
                        hb = h % 2
                        twq = sd.emit("gpsimd", (lambda hb, h: lambda gp: gp.dma_start(
                            out=wqt[hb][:], in_=wq[h, :, :]))(hb, h),
                            waits=[wqt_free[hb]], inc="dG", n=16)
                        sl, sfree = qkp.get()
                        for half in range(2):
                            for dc in range(16):
                                w0 = [twq, tok_xnown, sfree] if (half == 0 and dc == 0) else []
                                mm = sd.emit("tensor", (lambda sl, hb, half, dc: lambda t: t.matmul(
                                    psqk[sl][:, 512 * half:512 * (half + 1)],
                                    wqt[hb][:, 128 * dc:128 * (dc + 1)],
                                    xnown[:, 1024 * dc + 512 * half:1024 * dc + 512 * (half + 1)],
                                    start=(dc == 0), stop=(dc == 15)))(sl, hb, half, dc),
                                    waits=w0, inc="tt" if dc == 15 else None)
                        wqt_free[hb] = mm
                        tqe = sd.emit("vector", (lambda sl, h: lambda v: v.tensor_scalar_add(
                            out=qraw[:], in0=psqk[sl][:], scalar1=bq_s[:, h:h + 1]))(sl, h),
                            waits=[mm, last_qrope[0]], inc="vv")
                        qkp.put(sl, tqe)
                        tshf = sd.emit("vector", lambda v: v.stream_shuffle(
                            out=qswp[:], in_=qraw[:], mask=SWAP32), inc="vv")
                        sd.emit("gpsimd", lambda v: v.tensor_tensor(
                            out=qc1[:], in0=qswp[:], in1=sinq_s[:], op=OP.mult), waits=[tok_qt, tshf])
                        sd.emit("gpsimd", lambda v: v.tensor_tensor(
                            out=qc2[:], in0=qraw[:], in1=cosq_s[:], op=OP.mult))
                        tokqs[ln_] = sd.emit("gpsimd", (lambda ln_: lambda v: v.tensor_tensor(
                            out=qb[ln_][:], in0=qc1[:], in1=qc2[:], op=OP.add))(ln_), inc="gg")
                        last_qrope[0] = tokqs[ln_]

                    def tail_te(ln_, pws, te, p):
                        sl3, tfree = trp.get()
                        for i8 in range(8):
                            ttr = sd.emit("tensor", (lambda pws, sl3, i8: lambda t: t.transpose(
                                pstr[sl3][:, 128 * i8:128 * (i8 + 1)],
                                pwp[pws][:, 128 * i8:128 * (i8 + 1)], identb[:]))(pws, sl3, i8),
                                waits=[te, tfree] if i8 == 0 else [],
                                inc="tt" if i8 == 7 else None)
                        pwq.put(pws, ttr)
                        pts, ptfree = ptwp.get()
                        tpt = sd.emit("scalar", (lambda sl3, pts: lambda s_: s_.copy(
                            out=ptw[pts][:], in_=pstr[sl3][:]))(sl3, pts), waits=[ttr, ptfree], inc="ss")
                        trp.put(sl3, tpt)
                        return (pts, tpt, p)

                    def tail_pv(ln_, pts, tpt, p):
                        sl4, pfree4 = pvp.get()
                        for i8 in range(8):
                            tpv = sd.emit("tensor", (lambda pts, sl4, p, i8, gb: lambda t: t.matmul(
                                pspv[sl4][:, 0:128], ptw[pts][:, 128 * i8:128 * (i8 + 1)],
                                vg[gb][:, 1024 * p + 128 * i8:1024 * p + 128 * (i8 + 1)],
                                start=(i8 == 0), stop=(i8 == 7)))(pts, sl4, p, i8, gb),
                                waits=[tpt, tvg, pfree4] if i8 == 0 else [],
                                inc="tt" if i8 == 7 else None)
                        vg_free[gb] = tpv
                        ptwp.put(pts, tpv)
                        if p % 2 == 0:
                            ta = sd.emit("vector", (lambda ln_, sl4, p: lambda v: v.tensor_copy(
                                out=accw[ln_][:, 128 * p:128 * (p + 1)], in_=pspv[sl4][:, 0:128]))(ln_, sl4, p),
                                waits=[tpv, merge_tok[ln_]], inc="vv")
                        else:
                            ta = sd.emit("scalar", (lambda ln_, sl4, p: lambda s_: s_.copy(
                                out=accw[ln_][:, 128 * p:128 * (p + 1)], in_=pspv[sl4][:, 0:128]))(ln_, sl4, p),
                                waits=[tpv, merge_tok[ln_]], inc="ss")
                        merge_tok[ln_] = None
                        acc_toks_all[ln_].append(ta)
                        pvp.put(sl4, ta)

                    def do_merge(ln_, h, l, nP):
                        co = 16 * ln_
                        gv = sd.last("vv")
                        sd.emit("vector", (lambda ln_, co, nP: lambda v: v.tensor_reduce(
                            out=nmmin[:, ln_:ln_ + 1], in_=nmc[:, co:co + nP], axis=AX.X, op=OP.min))(ln_, co, nP),
                            waits=[gv], inc="vv")
                        g1 = sd.last("vv")
                        td8 = sd.emit("vector", (lambda ln_, co, nP: lambda v: v.tensor_single_scalar(
                            out=d8[:, co:co + nP], in_=nmc[:, co:co + nP],
                            scalar=nmmin[:, ln_:ln_ + 1], op=OP.subtract))(ln_, co, nP),
                            waits=[g1], inc="vv")
                        tu8 = sd.emit("scalar", (lambda co, nP: lambda s_: s_.activation(
                            out=u8[:, co:co + nP], in_=d8[:, co:co + nP], func=AF.Exp, scale=-1.0))(co, nP),
                            waits=[td8], inc="ss")
                        sd.emit("vector", (lambda co, nP: lambda v: v.tensor_tensor(
                            out=t8[:, co:co + nP], in0=u8[:, co:co + nP],
                            in1=denc[:, co:co + nP], op=OP.mult))(co, nP),
                            waits=[tu8], inc="vv")
                        g2 = sd.last("vv")
                        sd.emit("vector", (lambda ln_, co, nP: lambda v: v.reduce_sum(
                            out=denf[:, ln_:ln_ + 1], in_=t8[:, co:co + nP], axis=AX.X))(ln_, co, nP),
                            waits=[g2], inc="vv")
                        g3 = sd.last("vv")
                        sd.emit("vector", (lambda ln_: lambda v: v.reciprocal(
                            out=rden[:, ln_:ln_ + 1], in_=denf[:, ln_:ln_ + 1]))(ln_),
                            waits=[g3], inc="vv")
                        g4 = sd.last("vv")
                        sd.emit("vector", (lambda ln_, co, nP: lambda v: v.tensor_single_scalar(
                            out=u8b[:, co:co + nP], in_=u8[:, co:co + nP],
                            scalar=rden[:, ln_:ln_ + 1], op=OP.mult))(ln_, co, nP),
                            waits=[g4], inc="vv")
                        g5 = sd.last("vv")
                        acc_toks = [t for t in acc_toks_all[ln_] if t is not None]
                        vv_last = max((v for sm, v in acc_toks if sm == "vv"), default=None)
                        ss_last = max((v for sm, v in acc_toks if sm == "ss"), default=None)
                        accwait = [g5]
                        if vv_last is not None:
                            accwait.append(("vv", vv_last))
                        if ss_last is not None:
                            accwait.append(("ss", ss_last))
                        if nP == 1:
                            tat = sd.emit("vector", (lambda ln_, co: lambda v: v.tensor_single_scalar(
                                out=attt[ln_][:], in_=accw[ln_][:, 0:128],
                                scalar=u8b[:, co:co + 1], op=OP.mult))(ln_, co),
                                waits=accwait, inc="vv")
                        else:
                            sd.emit("vector", (lambda ln_, co: lambda v: v.tensor_single_scalar(
                                out=accf[ln_][:], in_=accw[ln_][:, 0:128],
                                scalar=u8b[:, co:co + 1], op=OP.mult))(ln_, co),
                                waits=accwait)
                            for p in range(1, nP - 1):
                                sd.emit("vector", (lambda ln_, co, p: lambda v: v.scalar_tensor_tensor(
                                    out=accf[ln_][:], in0=accw[ln_][:, 128 * p:128 * (p + 1)],
                                    scalar=u8b[:, co + p:co + p + 1], in1=accf[ln_][:],
                                    op0=OP.mult, op1=OP.add))(ln_, co, p))
                            tat = sd.emit("vector", (lambda ln_, co, nP: lambda v: v.scalar_tensor_tensor(
                                out=attt[ln_][:], in0=accw[ln_][:, 128 * (nP - 1):128 * nP],
                                scalar=u8b[:, co + nP - 1:co + nP], in1=accf[ln_][:],
                                op0=OP.mult, op1=OP.add))(ln_, co, nP), inc="vv")
                        acc_toks_all[ln_] = []
                        merge_tok[ln_] = tat
                        sl3, tfree = trp.get()
                        ttr = sd.emit("tensor", (lambda ln_, sl3: lambda t: t.transpose(
                            pstr[sl3][:, 0:128], attt[ln_][:], identb[:]))(ln_, sl3),
                            waits=[tat, tfree], inc="tt")
                        taf = sd.emit("scalar", (lambda sl3, h, l: lambda s_: s_.copy(
                            out=attfm[:, 1024 * h + 128 * l:1024 * h + 128 * (l + 1)],
                            in_=pstr[sl3][:, 0:128]))(sl3, h, l), waits=[ttr], inc="ss")
                        trp.put(sl3, taf)
                        return taf

                    def lane_gen(ln_, h):
                        nonlocal tok_attfm
                        tokq = tokqs[ln_]
                        for l in range(NLT):
                            nP = l + 1
                            pend = []
                            pend2 = []
                            for p in range(nP):
                                sl2, sfree2 = qkp.get()
                                sd.emit("tensor", (lambda sl2, ln_, l, p, gb: lambda t: t.matmul(
                                    psqk[sl2][:, 0:512], qb[ln_][:, 128 * l:128 * (l + 1)],
                                    kg[gb][:, 1024 * p:1024 * p + 512],
                                    start=True, stop=True))(sl2, ln_, l, p, gb),
                                    waits=[tokq, tkg, sfree2], inc=None)
                                tqk = sd.emit("tensor", (lambda sl2, ln_, l, p, gb: lambda t: t.matmul(
                                    psqk[sl2][:, 512:1024], qb[ln_][:, 128 * l:128 * (l + 1)],
                                    kg[gb][:, 1024 * p + 512:1024 * (p + 1)],
                                    start=True, stop=True))(sl2, ln_, l, p, gb), inc="tt")
                                kg_free[gb] = tqk
                                if p == nP - 1:
                                    sd.emit("vector", (lambda sl2: lambda v: v.tensor_tensor(
                                        out=psqk[sl2][:, 0:512], in0=psqk[sl2][:, 0:512],
                                        in1=mska_s[:], op=OP.add))(sl2), waits=[tqk, tok_masks])
                                    sd.emit("vector", (lambda sl2: lambda v: v.tensor_tensor(
                                        out=psqk[sl2][:, 512:1024], in0=psqk[sl2][:, 512:1024],
                                        in1=mskb_s[:], op=OP.add))(sl2))
                                    wred = []
                                else:
                                    wred = [tqk]
                                tm = sd.emit("vector", (lambda sl2, co, p: lambda v: v.reduce_max(
                                    nmc[:, co + p:co + p + 1], psqk[sl2][:], axis=AX.X,
                                    negate=True))(sl2, 16 * ln_, p),
                                    waits=wred, inc="vv")
                                pws, pwfree = pwq.get()
                                te = sd.emit("scalar", (lambda sl2, pws, co, p: lambda s_: s_.activation(
                                    out=pwp[pws][:], in_=psqk[sl2][:], func=AF.Exp,
                                    bias=nmc[:, co + p:co + p + 1], scale=1.0,
                                    accum_out=denc[:, co + p:co + p + 1]))(sl2, pws, 16 * ln_, p),
                                    waits=[tm, pwfree], inc="ss")
                                qkp.put(sl2, te)
                                pend.append((pws, te, p))
                                if len(pend) > 2:
                                    pend2.append(tail_te(ln_, *pend.pop(0)))
                                if len(pend2) > 1:
                                    tail_pv(ln_, *pend2.pop(0))
                                yield
                            while pend:
                                pend2.append(tail_te(ln_, *pend.pop(0)))
                            while pend2:
                                tail_pv(ln_, *pend2.pop(0))
                            tok_attfm = do_merge(ln_, h, l, nP)
                            yield

                    gens = [lane_gen(0, heads[0]), lane_gen(1, heads[1])]
                    alive = [True, True]
                    while alive[0] or alive[1]:
                        for gi in (0, 1):
                            if alive[gi]:
                                try:
                                    next(gens[gi])
                                except StopIteration:
                                    alive[gi] = False

        sd.barrier()
        tok_fsp = tok_fsp_last[0]
        with ExitStack() as s3:
            e3 = s3.enter_context
            outfm = e3(nc.sbuf_tensor("outfm", [128, 16 * 1024], F32))
            wob = [e3(nc.sbuf_tensor(f"wob{i}", [128, 2048], BF16)) for i in range(2)]
            ffnrt = [e3(nc.sbuf_tensor(f"ffnrt{i}", [128, 1024], F32)) for i in range(2)]
            psA = [e3(nc.psum_tensor(f"psA{i}", [128, 512], F32)) for i in range(2)]

            wob_free = [None, None]
            ffnrt_free = [None, None]
            psa = Pool(2)
            for oc in range(16):
                ob = oc % 2
                two = sd.emit("gpsimd", (lambda ob, oc: lambda gp: gp.dma_start(
                    out=wob[ob][:], in_=wo[oc, :, :]))(ob, oc),
                    waits=[wob_free[ob]], inc="dG", n=16)
                tfr = sd.emit("sync", (lambda ob, oc: lambda q: q.dma_start(
                    out=ffnrt[ob][:], in_=fdr[:, 1024 * oc:1024 * (oc + 1)]))(ob, oc),
                    waits=[tok_fsp, ffnrt_free[ob]], inc="dK", n=16)
                for half in range(2):
                    pj, pfree = psa.get()
                    for ec in range(16):
                        w0 = [two, tok_attfm, pfree] if ec == 0 else []
                        mm = sd.emit("tensor", (lambda pj, ob, ec, half: lambda t: t.matmul(
                            psA[pj][:], wob[ob][:, 128 * ec:128 * (ec + 1)],
                            attfm[:, 1024 * ec + 512 * half:1024 * ec + 512 * (half + 1)],
                            start=(ec == 0), stop=(ec == 15)))(pj, ob, ec, half),
                            waits=w0, inc="tt" if ec == 15 else None)
                    tev = sd.emit("vector", (lambda pj, ob, oc, half: lambda v: v.scalar_tensor_tensor(
                        out=outfm[:, 1024 * oc + 512 * half:1024 * oc + 512 * (half + 1)],
                        in0=psA[pj][:], scalar=bo_s[:, oc:oc + 1],
                        in1=ffnrt[ob][:, 512 * half:512 * (half + 1)],
                        op0=OP.add, op1=OP.add))(pj, ob, oc, half),
                        waits=[mm, tfr], inc="vv")
                    psa.put(pj, tev)
                    ffnrt_free[ob] = tev
                wob_free[ob] = mm
            tok_p2 = sd.last("vv")
            xot = [e3(nc.sbuf_tensor(f"xot{i}", [128, 2048], F32)) for i in range(2)]
            otm = [e3(nc.sbuf_tensor(f"otm{i}", [128, 2048], F32)) for i in range(2)]
            pse = [e3(nc.psum_tensor(f"pse{i}", [128, 512], F32)) for i in range(2)]
            xot_free = [None, None]
            otm_free = [None, None]
            pse_pool = Pool(2)
            for l in range(NLT):
                lb = l % 2
                txo = sd.emit("sync", (lambda l, lb: lambda q: q.dma_start(
                    out=xot[lb][:], in_=xown[l, :, :]))(l, lb),
                    waits=[xot_free[lb]], inc="dX", n=16)
                tadds = []
                for b4 in range(4):
                    pj, pfree = pse_pool.get()
                    for j in range(4):
                        oc = 4 * b4 + j
                        ttr = sd.emit("tensor", (lambda pj, oc, l, j: lambda t: t.transpose(
                            pse[pj][:, 128 * j:128 * (j + 1)],
                            outfm[:, 1024 * oc + 128 * l:1024 * oc + 128 * (l + 1)], identf[:]))(pj, oc, l, j),
                            waits=[tok_p2, pfree] if j == 0 else [],
                            inc="tt" if j == 3 else None)
                    tadd = sd.emit("vector", (lambda pj, lb, b4: lambda v: v.tensor_tensor(
                        out=otm[lb][:, 512 * b4:512 * (b4 + 1)], in0=pse[pj][:],
                        in1=xot[lb][:, 512 * b4:512 * (b4 + 1)], op=OP.add))(pj, lb, b4),
                        waits=[ttr, txo] + ([otm_free[lb]] if b4 == 0 else []), inc="vv")
                    pse_pool.put(pj, tadd)
                    tadds.append(tadd)
                xot_free[lb] = tadds[-1]
                tst = sd.emit("sync", (lambda l, lb: lambda q: q.dma_start(
                    out=oout[l, :, :], in_=otm[lb][:]))(l, lb),
                    waits=[tadds[-1]], inc="dO", n=16)
                otm_free[lb] = tst

            fin = sd.last("dO")
            sd.emit("sync", lambda q: q.nop(), waits=[fin])

            semnames = sorted(sd.cnt.keys()) + [s for s in ("dM",) if s not in sd.cnt]
            with ExitStack() as sems_stack:
                sems = {s: sems_stack.enter_context(nc.semaphore(s)) for s in set(semnames)}
                with nc.Block() as block:
                    @block.sync
                    def _(q):
                        sd.replay("sync", q, sems)

                    @block.vector
                    def _(v):
                        sd.replay("vector", v, sems)

                    @block.scalar
                    def _(s):
                        sd.replay("scalar", s, sems)

                    @block.tensor
                    def _(t):
                        sd.replay("tensor", t, sems)

                    @block.gpsimd
                    def _(gp):
                        sd.replay("gpsimd", gp, sems)
        sA.close()
    return nc


def _tables():
    inv = (np.float32(1.0) /
           (np.float32(10000.0) ** (np.arange(0, HD, 2, dtype=np.float32) / np.float32(HD))))
    freqs = np.arange(S, dtype=np.float32)[:, None] * inv[None, :].astype(np.float32)
    cos = np.cos(freqs.astype(np.float64)).astype(np.float32)
    sin = np.sin(freqs.astype(np.float64)).astype(np.float32)
    cos_fm = np.repeat(cos.T, 2, axis=0)
    sgn = np.where(np.arange(HD) % 2 == 0, -1.0, 1.0).astype(np.float32)[:, None]
    sin_fm = np.repeat(sin.T, 2, axis=0) * sgn
    return cos_fm.astype(np.float32), sin_fm.astype(np.float32)


_NC_CACHE = {}


def kernel(L0_attn_x, L0_attn_Wq, L0_attn_bq, L0_attn_Wk, L0_attn_bk,
           L0_attn_Wv, L0_attn_bv, L0_attn_Wo, L0_attn_bo,
           L0_ffn_up_weight, L0_ffn_down_weight, L0_ffn_up_scale,
           L0_ffn_down_scale, L0_ffn_output_scale):
    f32 = np.float32
    x = np.ascontiguousarray(np.asarray(L0_attn_x, f32).reshape(S, D))
    wq_b = np.ascontiguousarray(np.asarray(L0_attn_Wq, f32).reshape(NH, HD, 16, 128).transpose(0, 3, 2, 1))
    wk_t = np.ascontiguousarray(np.asarray(L0_attn_Wk, f32).reshape(512, 16, 128).transpose(2, 1, 0))
    wv_t = np.ascontiguousarray(np.asarray(L0_attn_Wv, f32).reshape(512, 16, 128).transpose(2, 1, 0))
    wo_b = np.ascontiguousarray(np.asarray(L0_attn_Wo, f32).reshape(16, 128, 16, 128).transpose(0, 3, 2, 1))
    wup_b = np.ascontiguousarray(np.asarray(L0_ffn_up_weight, f32).reshape(4, 16, 128, 16, 128).transpose(0, 1, 4, 3, 2))
    wd_b = np.ascontiguousarray(np.asarray(L0_ffn_down_weight, f32).reshape(16, 128, 4, 16, 128).transpose(2, 0, 4, 3, 1))
    bq_c = np.ascontiguousarray(np.asarray(L0_attn_bq, f32).reshape(16, 128).T)
    bk_c = np.ascontiguousarray(np.asarray(L0_attn_bk, f32).reshape(4, 128).T)
    bv_c = np.ascontiguousarray(np.asarray(L0_attn_bv, f32).reshape(4, 128).T)
    bo_c = np.ascontiguousarray(np.asarray(L0_attn_bo, f32).reshape(16, 128).T)
    up_c = np.ascontiguousarray(np.asarray(L0_ffn_up_scale, f32).reshape(64, 128).T)
    ds_c = np.ascontiguousarray(np.asarray(L0_ffn_down_scale, f32).reshape(16, 128).T)
    os_r = np.ascontiguousarray(np.broadcast_to(np.asarray(L0_ffn_output_scale, f32).reshape(1, 1), (128, 16)))
    cos_fm, sin_fm = _tables()
    identb = np.eye(128, dtype=ml_dtypes.bfloat16)
    identf = np.eye(128, dtype=f32)

    if "nc" not in _NC_CACHE:
        _NC_CACHE["nc"] = build_nc()
    nc = _NC_CACHE["nc"]

    jcol = np.arange(512)[None, :]
    prow = np.arange(128)[:, None]
    in_maps = []
    own_rows_all = []
    for c in range(NC):
        own = np.concatenate([(8 * l + c) * 128 + np.arange(128) for l in range(NLT)])
        own_rows_all.append(own)
        mska = np.where(jcol <= 128 * c + prow, 0.0, NEGBIG).astype(f32)
        mskb = np.where(jcol <= 128 * c + prow - 512, 0.0, NEGBIG).astype(f32)
        in_maps.append(dict(
            xfull=x, xown=np.ascontiguousarray(x[own].reshape(NLT, 128, D)),
            wq=wq_b, wk=wk_t, wv=wv_t, wo=wo_b, wup=wup_b, wd=wd_b,
            bqc=bq_c, bkc=bk_c, bvc=bv_c, boc=bo_c,
            upsc=up_c, dsc=ds_c, osc=os_r,
            cosk=cos_fm, sink=sin_fm,
            cosq=np.ascontiguousarray((ALPHA * cos_fm)[:, own]),
            sinq=np.ascontiguousarray((ALPHA * sin_fm)[:, own]),
            mska=mska, mskb=mskb, idb=identb, idf=identf,
        ))
    res = run_bass_kernel_spmd(nc, in_maps, core_ids=list(range(NC)))
    out = np.empty((S, D), f32)
    for c in range(NC):
        out[own_rows_all[c]] = np.asarray(res.results[c]["oout"], f32).reshape(NLT * 128, D)
    return out.reshape(1, S, D)


# version 28 (final)
# speedup vs baseline: 1.2690x; 1.2690x over previous
import math
from contextlib import ExitStack

import numpy as np
import ml_dtypes

import concourse.bass as bass
import concourse.mybir as mybir
from concourse.bass_utils import run_bass_kernel_spmd

F32 = mybir.dt.float32
BF16 = mybir.dt.bfloat16
AF = mybir.ActivationFunctionType
OP = mybir.AluOpType
AX = mybir.AxisListType

S, D, NH, HD, NKV = 8192, 2048, 16, 128, 4
DHID = 8192
NC = 8
NLT = 8
ALPHA = 1.0 / math.sqrt(HD)
NEGBIG = -3.0e37

ENGS = ("sync", "vector", "scalar", "tensor", "gpsimd")


class Sched:

    def __init__(self):
        self.ops = {e: [] for e in ENGS}
        self.cnt = {}

    def emit(self, eng, fn, waits=(), inc=None, n=1):
        waits = [w for w in waits if w is not None]
        self.ops[eng].append((fn, waits, inc, n))
        if inc is None:
            return None
        self.cnt[inc] = self.cnt.get(inc, 0) + n
        return (inc, self.cnt[inc])

    def last(self, sem):
        return (sem, self.cnt.get(sem, 0))

    def barrier(self):
        toks = [self.last(s) for s in list(self.cnt.keys())]
        for eng in ENGS:
            self.emit(eng, lambda x: x.nop(), waits=toks, inc={"sync": None}.get(eng))

    def replay(self, name, eng, sems):
        for fn, waits, inc, n in self.ops[name]:
            for s, v in waits:
                eng.wait_ge(sems[s], v)
            ins = fn(eng)
            if inc is not None:
                ins.then_inc(sems[inc], n)


class Pool:

    def __init__(self, k):
        self.k = k
        self.free = [None] * k
        self.i = 0

    def get(self):
        j = self.i % self.k
        self.i += 1
        return j, self.free[j]

    def put(self, j, tok):
        self.free[j] = tok


def build_nc():
    nc = bass.Bass("TRN2", target_bir_lowering=False, debug=False)
    dt = nc.dram_tensor
    xfull = dt("xfull", [S, D], F32, kind="ExternalInput").ap()
    xown = dt("xown", [NLT, 128, D], F32, kind="ExternalInput").ap()
    wq = dt("wq", [NH, 128, 2048], F32, kind="ExternalInput").ap()
    wk = dt("wk", [128, 16 * 512], F32, kind="ExternalInput").ap()
    wv = dt("wv", [128, 16 * 512], F32, kind="ExternalInput").ap()
    wo = dt("wo", [16, 128, 2048], F32, kind="ExternalInput").ap()
    wup = dt("wup", [4, 16, 128, 2048], F32, kind="ExternalInput").ap()
    wd = dt("wd", [4, 16, 128, 2048], F32, kind="ExternalInput").ap()
    bqc = dt("bqc", [128, 16], F32, kind="ExternalInput").ap()
    bkc = dt("bkc", [128, 4], F32, kind="ExternalInput").ap()
    bvc = dt("bvc", [128, 4], F32, kind="ExternalInput").ap()
    boc = dt("boc", [128, 16], F32, kind="ExternalInput").ap()
    upsc = dt("upsc", [128, 64], F32, kind="ExternalInput").ap()
    dsc = dt("dsc", [128, 16], F32, kind="ExternalInput").ap()
    osc = dt("osc", [128, 16], F32, kind="ExternalInput").ap()
    cosk = dt("cosk", [128, S], F32, kind="ExternalInput").ap()
    sink = dt("sink", [128, S], F32, kind="ExternalInput").ap()
    cosq = dt("cosq", [128, 1024], F32, kind="ExternalInput").ap()
    sinq = dt("sinq", [128, 1024], F32, kind="ExternalInput").ap()
    mska = dt("mska", [128, 512], F32, kind="ExternalInput").ap()
    mskb = dt("mskb", [128, 512], F32, kind="ExternalInput").ap()
    idb = dt("idb", [128, 128], BF16, kind="ExternalInput").ap()
    idf = dt("idf", [128, 128], F32, kind="ExternalInput").ap()
    kdr = dt("kdr", [NKV, 128, S], BF16, kind="Internal").ap()
    vdr = dt("vdr", [NKV, 128, S], BF16, kind="Internal").ap()
    fdr = dt("fdr", [128, 16 * 1024], F32, kind="Internal").ap()
    oout = dt("oout", [NLT, 128, D], F32, kind="ExternalOutput").ap()

    sd = Sched()
    SWAP32 = [(i + 1 if i % 2 == 0 else i - 1) for i in range(32)]

    with ExitStack() as top:
        e = top.enter_context
        identb = e(nc.sbuf_tensor("identb", [128, 128], BF16))
        identf = e(nc.sbuf_tensor("identf", [128, 128], F32))
        bq_s = e(nc.sbuf_tensor("bq_s", [128, 16], F32))
        bk_s = e(nc.sbuf_tensor("bk_s", [128, 4], F32))
        bv_s = e(nc.sbuf_tensor("bv_s", [128, 4], F32))
        bo_s = e(nc.sbuf_tensor("bo_s", [128, 16], F32))
        upsc_s = e(nc.sbuf_tensor("upsc_s", [128, 64], F32))
        dsc_s = e(nc.sbuf_tensor("dsc_s", [128, 16], F32))
        osc_s = e(nc.sbuf_tensor("osc_s", [128, 16], F32))
        dsos_s = e(nc.sbuf_tensor("dsos_s", [128, 16], F32))
        mska_s = e(nc.sbuf_tensor("mska_s", [128, 512], F32))
        mskb_s = e(nc.sbuf_tensor("mskb_s", [128, 512], F32))
        xnown = e(nc.sbuf_tensor("xnown", [128, 16 * 1024], BF16))

        for dst, src in ((identb, idb), (identf, idf), (bq_s, bqc), (bk_s, bkc),
                         (bv_s, bvc), (bo_s, boc), (upsc_s, upsc), (dsc_s, dsc),
                         (osc_s, osc), (mska_s, mska), (mskb_s, mskb)):
            sd.emit("sync", (lambda d, s: lambda g: g.dma_start(out=d[:], in_=s))(dst, src),
                    inc="dM", n=16)
        tok_consts = sd.last("dM")
        tok_dsos = sd.emit("vector", lambda v: v.tensor_tensor(
            out=dsos_s[:], in0=dsc_s[:], in1=osc_s[:], op=OP.mult), waits=[tok_consts], inc="vv")
        sd.emit("tensor", lambda t: t.nop(), waits=[tok_consts])
        sd.emit("scalar", lambda s_: s_.nop(), waits=[tok_consts])

        with ExitStack() as s1:
            e1 = s1.enter_context
            coss = [e1(nc.sbuf_tensor(f"coss{i}", [128, 512], F32)) for i in range(2)]
            sins = [e1(nc.sbuf_tensor(f"sins{i}", [128, 512], F32)) for i in range(2)]
            wkt = e1(nc.sbuf_tensor("wkt", [128, 16 * 512], BF16))
            wvt = e1(nc.sbuf_tensor("wvt", [128, 16 * 512], BF16))
            xf = [e1(nc.sbuf_tensor(f"xf{i}", [128, D], F32)) for i in range(2)]
            xnt = [e1(nc.sbuf_tensor(f"xnt{i}", [128, D], BF16)) for i in range(2)]
            xnf = [e1(nc.sbuf_tensor(f"xnf{i}", [128, 16 * 512], BF16)) for i in range(2)]
            bn6 = [e1(nc.sbuf_tensor(f"bn6{i}", [128, 24], F32)) for i in range(2)]
            bn2 = [e1(nc.sbuf_tensor(f"bn2{i}", [128, 2], F32)) for i in range(2)]
            varp = [e1(nc.sbuf_tensor(f"varp{i}", [128, 1], F32)) for i in range(2)]
            rvar = [e1(nc.sbuf_tensor(f"rvar{i}", [128, 1], F32)) for i in range(2)]
            rstd = [e1(nc.sbuf_tensor(f"rstd{i}", [128, 1], F32)) for i in range(2)]
            nmr = [e1(nc.sbuf_tensor(f"nmr{i}", [128, 1], F32)) for i in range(2)]
            kraw = e1(nc.sbuf_tensor("kraw", [128, 512], F32))
            kswp = e1(nc.sbuf_tensor("kswp", [128, 512], F32))
            kc1 = e1(nc.sbuf_tensor("kc1", [128, 512], F32))
            kc2 = e1(nc.sbuf_tensor("kc2", [128, 512], F32))
            kbf = [e1(nc.sbuf_tensor(f"kbf{i}", [128, 512], BF16)) for i in range(2)]
            vb16 = [e1(nc.sbuf_tensor(f"vb16{i}", [128, 512], BF16)) for i in range(2)]
            vkm = [e1(nc.sbuf_tensor(f"vkm{i}", [128, 512], BF16)) for i in range(2)]
            ps_xn = [e1(nc.psum_tensor(f"ps_xn{i}", [128, 2048], BF16)) for i in range(2)]
            ps_kv = [e1(nc.psum_tensor(f"ps_kv{i}", [128, 512], F32)) for i in range(2)]
            ps_vt = e1(nc.psum_tensor("ps_vt", [128, 1024], BF16))

            tok_wkt = sd.emit("gpsimd", lambda g: g.dma_start(out=wkt[:], in_=wk), inc="dG", n=16)
            tok_wvt = sd.emit("gpsimd", lambda g: g.dma_start(out=wvt[:], in_=wv), inc="dG", n=16)

            xf_free = [None, None]
            xnt_free = [None, None]
            psxn_free = [None, None]
            lnb_free = [None, None]

            def ln_tile(i, src, dst_fn, evac_waits=()):
                b = i % len(xf)
                bb = i % 2
                tx = sd.emit("sync", lambda g: g.dma_start(out=xf[b][:], in_=src),
                             waits=[xf_free[b]], inc="dX", n=16)
                for j in range(4):
                    ins = sd.emit("vector", (lambda b, bb, j: lambda v: v.bn_stats(
                        out=bn6[bb][:, 6 * j:6 * j + 6], in_=xf[b][:, 512 * j:512 * (j + 1)]))(b, bb, j),
                        waits=[tx, lnb_free[bb]] if j == 0 else [], inc="vv" if j == 3 else None)
                sd.emit("vector", (lambda bb: lambda v: v.bn_aggr(out=bn2[bb][:], in_=bn6[bb][:]))(bb),
                        waits=[ins], inc="vv")
                g1 = sd.last("vv")
                sd.emit("vector", (lambda bb: lambda v: v.tensor_scalar_add(
                    out=varp[bb][:], in0=bn2[bb][:, 1:2], scalar1=1e-5))(bb), waits=[g1], inc="vv")
                g2 = sd.last("vv")
                tv = sd.emit("vector", (lambda bb: lambda v: v.reciprocal(
                    out=rvar[bb][:], in_=varp[bb][:]))(bb), waits=[g2], inc="vv")
                ts_ = sd.emit("scalar", (lambda bb: lambda s: s.activation(
                    out=rstd[bb][:], in_=rvar[bb][:], func=AF.Sqrt))(bb), waits=[tv], inc="ss")
                sd.emit("vector", (lambda bb: lambda v: v.tensor_tensor(
                    out=nmr[bb][:], in0=bn2[bb][:, 0:1], in1=rstd[bb][:], op=OP.mult))(bb),
                    waits=[ts_], inc="vv")
                g3 = sd.last("vv")
                tn = sd.emit("vector", (lambda bb: lambda v: v.tensor_scalar_mul(
                    nmr[bb][:], nmr[bb][:], -1.0))(bb), waits=[g3], inc="vv")
                lnb_free[bb] = tn
                tnorm = sd.emit("scalar", (lambda b, bb: lambda s: s.activation(
                    out=xnt[b][:], in_=xf[b][:], func=AF.Identity,
                    bias=nmr[bb][:], scale=rstd[bb][:]))(b, bb),
                    waits=[tn, xnt_free[b]], inc="ss")
                xf_free[b] = tnorm
                for j in range(16):
                    tt_ = sd.emit("tensor", (lambda b, bb, j: lambda t: t.transpose(
                        ps_xn[bb][:, 128 * j:128 * (j + 1)], xnt[b][:, 128 * j:128 * (j + 1)],
                        identb[:]))(b, bb, j),
                        waits=[tnorm, psxn_free[bb]] if j == 0 else [],
                        inc="tt" if j == 15 else None)
                xnt_free[b] = tt_
                dst = dst_fn()
                tev = sd.emit("scalar", (lambda bb, dst: lambda s: s.copy(
                    out=dst, in_=ps_xn[bb].ap().rearrange("p (a n) -> p a n", n=128)))(bb, dst),
                    waits=[tt_] + list(evac_waits), inc="ss")
                psxn_free[bb] = tev
                return tev

            xnf_free = [None, None]
            last_krope = [None]
            kv_mm_last = [None, None]
            cos_free = [None, None]
            pskv = Pool(2)
            spill_kbf = Pool(2)
            spill_v = Pool(2)
            psvt_free = None
            tok_spill = []

            xnf3 = [xnf[i].ap().rearrange("p (a n) -> p a n", n=512) for i in range(2)]

            for g in range(16):
                gb = g % 2
                tca = sd.emit("sync", (lambda g, gb: lambda q: q.dma_start(
                    out=coss[gb][:], in_=cosk[:, 512 * g:512 * (g + 1)]))(g, gb),
                    waits=[cos_free[gb]], inc="dT", n=16)
                tcb = sd.emit("sync", (lambda g, gb: lambda q: q.dma_start(
                    out=sins[gb][:], in_=sink[:, 512 * g:512 * (g + 1)]))(g, gb), inc="dT", n=16)
                evs = []
                for tt4 in range(4):
                    i = 4 * g + tt4
                    dst_fn = (lambda gb, tt4: lambda: xnf3[gb][:, :, 128 * tt4:128 * (tt4 + 1)])(gb, tt4)
                    ew = [xnf_free[gb]] if tt4 == 0 else []
                    evs.append(ln_tile(i, xfull[128 * i:128 * (i + 1), :], dst_fn, evac_waits=ew))
                tok_grp = evs[-1]
                for kvc in range(NKV):
                    pj, pfree = pskv.get()
                    for dc in range(16):
                        w0 = [tok_grp, tok_wkt, pfree] if dc == 0 else []
                        mm = sd.emit("tensor", (lambda pj, dc, kvc, gb: lambda t: t.matmul(
                            ps_kv[pj][:], wkt[:, 512 * dc + 128 * kvc:512 * dc + 128 * (kvc + 1)],
                            xnf[gb][:, 512 * dc:512 * (dc + 1)],
                            start=(dc == 0), stop=(dc == 15)))(pj, dc, kvc, gb),
                            waits=w0, inc="tt" if dc == 15 else None)
                    kv_mm_last[gb] = mm
                    tev = sd.emit("vector", (lambda pj, kvc: lambda v: v.tensor_scalar_add(
                        out=kraw[:], in0=ps_kv[pj][:], scalar1=bk_s[:, kvc:kvc + 1]))(pj, kvc),
                        waits=[mm, last_krope[0]], inc="vv")
                    pskv.put(pj, tev)
                    tshf = sd.emit("vector", lambda v: v.stream_shuffle(
                        out=kswp[:], in_=kraw[:], mask=SWAP32), inc="vv")
                    sd.emit("gpsimd", (lambda gb: lambda v: v.tensor_tensor(
                        out=kc1[:], in0=kswp[:], in1=sins[gb][:], op=OP.mult))(gb),
                        waits=[tcb, tshf])
                    tkc2 = sd.emit("gpsimd", (lambda gb: lambda v: v.tensor_tensor(
                        out=kc2[:], in0=kraw[:], in1=coss[gb][:], op=OP.mult))(gb),
                        waits=[tca], inc="gg" if kvc == NKV - 1 else None)
                    if kvc == NKV - 1:
                        cos_free[gb] = tkc2
                    sb, sfree = spill_kbf.get()
                    tkb = sd.emit("gpsimd", (lambda sb: lambda v: v.tensor_tensor(
                        out=kbf[sb][:], in0=kc1[:], in1=kc2[:], op=OP.add))(sb),
                        waits=[sfree], inc="gg")
                    last_krope[0] = tkb
                    tsp = sd.emit("sync", (lambda sb, kvc, g: lambda q: q.dma_start(
                        out=kdr[kvc, :, 512 * g:512 * (g + 1)], in_=kbf[sb][:]))(sb, kvc, g),
                        waits=[tkb], inc="dS", n=16)
                    spill_kbf.put(sb, tsp)
                    tok_spill.append(tsp)
                    pj, pfree = pskv.get()
                    for dc in range(16):
                        w0 = [pfree] if dc == 0 else []
                        mm = sd.emit("tensor", (lambda pj, dc, kvc, gb: lambda t: t.matmul(
                            ps_kv[pj][:], wvt[:, 512 * dc + 128 * kvc:512 * dc + 128 * (kvc + 1)],
                            xnf[gb][:, 512 * dc:512 * (dc + 1)],
                            start=(dc == 0), stop=(dc == 15)))(pj, dc, kvc, gb),
                            waits=w0 + ([tok_wvt] if (g == 0 and kvc == 0 and dc == 0) else []),
                            inc="tt" if dc == 15 else None)
                    kv_mm_last[gb] = mm
                    vbj, vbfree = spill_v.get()
                    tvb = sd.emit("vector", (lambda pj, kvc, vbj: lambda v: v.tensor_scalar_add(
                        out=vb16[vbj][:], in0=ps_kv[pj][:], scalar1=bv_s[:, kvc:kvc + 1]))(pj, kvc, vbj),
                        waits=[mm, vbfree], inc="vv")
                    pskv.put(pj, tvb)
                    for bb in range(4):
                        ttr = sd.emit("tensor", (lambda vbj, bb: lambda t: t.transpose(
                            ps_vt[:, 128 * bb:128 * (bb + 1)], vb16[vbj][:, 128 * bb:128 * (bb + 1)],
                            identb[:]))(vbj, bb),
                            waits=[tvb, psvt_free] if bb == 0 else [],
                            inc="tt" if bb == 3 else None)
                    tvk = sd.emit("vector", (lambda vbj: lambda v: v.tensor_copy(
                        out=vkm[vbj][:], in_=ps_vt[:, 0:512]))(vbj), waits=[ttr], inc="vv")
                    psvt_free = tvk
                    tsp = sd.emit("sync", (lambda vbj, kvc, g: lambda q: q.dma_start(
                        out=vdr[kvc, :, 512 * g:512 * (g + 1)], in_=vkm[vbj][:]))(vbj, kvc, g),
                        waits=[tvk], inc="dS", n=16)
                    spill_v.put(vbj, tsp)
                    tok_spill.append(tsp)
                xnf_free[gb] = mm

            xnown3 = xnown.ap().rearrange("p (a n) -> p a n", n=1024)
            tok_xnown = None
            for l in range(NLT):
                dst_fn = (lambda l: lambda: xnown3[:, :, 128 * l:128 * (l + 1)])(l)
                tok_xnown = ln_tile(64 + l, xown[l, :, :], dst_fn)

        sd.barrier()
        with ExitStack() as sf_:
            ef = sf_.enter_context
            ffnacc = ef(nc.sbuf_tensor("ffnacc", [128, 16 * 1024], F32))
            psA = [ef(nc.psum_tensor(f"psAf{i}", [128, 512], F32)) for i in range(2)]
            psB = [ef(nc.psum_tensor(f"psBf{i}", [128, 512], F32)) for i in range(2)]
            psa = Pool(2)
            psb = Pool(2)
            wuf_free = [None, None]
            wub_free = [None, None]
            wdf_free = [None, None]
            wdb_free = [None, None]
            wuf = [ef(nc.sbuf_tensor(f"wuf{i}", [128, 2048], F32)) for i in range(2)]
            wub = [ef(nc.sbuf_tensor(f"wub{i}", [128, 2048], BF16)) for i in range(2)]
            hjb = ef(nc.sbuf_tensor("hjb", [128, 16 * 1024], BF16))
            wdf = [ef(nc.sbuf_tensor(f"wdf{i}", [128, 2048], F32)) for i in range(2)]
            wdb = [ef(nc.sbuf_tensor(f"wdb{i}", [128, 2048], BF16)) for i in range(2)]
            hjb_used = None
            for jb in range(4):
                for jc in range(16):
                    cb = jc % 2
                    twf = sd.emit("sync", (lambda jb, jc, cb: lambda q: q.dma_start(
                        out=wuf[cb][:], in_=wup[jb, jc, :, :]))(jb, jc, cb),
                        waits=[wuf_free[cb]], inc="dW", n=16)
                    tsg = sd.emit("scalar", (lambda cb: lambda s: s.sign(
                        out=wub[cb][:], in_=wuf[cb][:]))(cb),
                        waits=[twf, wub_free[cb]], inc="ss")
                    wuf_free[cb] = tsg
                    for half in range(2):
                        pj, pfree = psa.get()
                        for dc in range(16):
                            w0 = ([tsg, pfree, tok_xnown] + ([hjb_used] if (dc == 0 and half == 0) else [])) if dc == 0 else []
                            mm = sd.emit("tensor", (lambda pj, cb, dc, half: lambda t: t.matmul(
                                psA[pj][:], wub[cb][:, 128 * dc:128 * (dc + 1)],
                                xnown[:, 1024 * dc + 512 * half:1024 * dc + 512 * (half + 1)],
                                start=(dc == 0), stop=(dc == 15)))(pj, cb, dc, half),
                                waits=w0, inc="tt" if dc == 15 else None)
                        tev = sd.emit("vector", (lambda pj, jb, jc, half: lambda v: v.tensor_scalar(
                            out=hjb[:, 1024 * jc + 512 * half:1024 * jc + 512 * (half + 1)],
                            in0=psA[pj][:], scalar1=upsc_s[:, 16 * jb + jc:16 * jb + jc + 1],
                            scalar2=0.0, op0=OP.mult, op1=OP.max))(pj, jb, jc, half),
                            waits=[mm], inc="vv")
                        psa.put(pj, tev)
                    wub_free[cb] = mm
                tok_h = sd.last("vv")
                for oc in range(16):
                    cb = oc % 2
                    twf = sd.emit("sync", (lambda jb, oc, cb: lambda q: q.dma_start(
                        out=wdf[cb][:], in_=wd[jb, oc, :, :]))(jb, oc, cb),
                        waits=[wdf_free[cb]], inc="dW", n=16)
                    tsg = sd.emit("scalar", (lambda cb: lambda s: s.sign(
                        out=wdb[cb][:], in_=wdf[cb][:]))(cb),
                        waits=[twf, wdb_free[cb]], inc="ss")
                    wdf_free[cb] = tsg
                    for half in range(2):
                        pj, pfree = psb.get()
                        for jc in range(16):
                            w0 = [tsg, tok_h, pfree] if jc == 0 else []
                            mm = sd.emit("tensor", (lambda pj, cb, jc, half: lambda t: t.matmul(
                                psB[pj][:], wdb[cb][:, 128 * jc:128 * (jc + 1)],
                                hjb[:, 1024 * jc + 512 * half:1024 * jc + 512 * (half + 1)],
                                start=(jc == 0), stop=(jc == 15)))(pj, cb, jc, half),
                                waits=w0, inc="tt" if jc == 15 else None)
                        if jb == 0:
                            tev = sd.emit("vector", (lambda pj, oc, half: lambda v: v.tensor_single_scalar(
                                out=ffnacc[:, 1024 * oc + 512 * half:1024 * oc + 512 * (half + 1)],
                                in_=psB[pj][:], scalar=dsos_s[:, oc:oc + 1], op=OP.mult))(pj, oc, half),
                                waits=[mm] + ([tok_dsos] if (oc == 0 and half == 0) else []),
                                inc="vv")
                        else:
                            tev = sd.emit("vector", (lambda pj, oc, half: lambda v: v.scalar_tensor_tensor(
                                out=ffnacc[:, 1024 * oc + 512 * half:1024 * oc + 512 * (half + 1)],
                                in0=psB[pj][:], scalar=dsos_s[:, oc:oc + 1],
                                in1=ffnacc[:, 1024 * oc + 512 * half:1024 * oc + 512 * (half + 1)],
                                op0=OP.mult, op1=OP.add))(pj, oc, half),
                                waits=[mm], inc="vv")
                        psb.put(pj, tev)
                    wdb_free[cb] = mm
                hjb_used = mm

            tok_ffn_acc = sd.last("vv")
            tok_fsp = sd.emit("sync", lambda q: q.dma_start(out=fdr, in_=ffnacc[:]),
                              waits=[tok_ffn_acc], inc="dS", n=16)
        sd.barrier()
        tok_spill_all = sd.last("dS")
        sA = ExitStack()
        attfm = sA.enter_context(nc.sbuf_tensor("attfm", [128, 16 * 1024], BF16))
        with ExitStack() as s2:
            e2 = s2.enter_context
            cosq_s = e2(nc.sbuf_tensor("cosq_s", [128, 1024], F32))
            sinq_s = e2(nc.sbuf_tensor("sinq_s", [128, 1024], F32))
            kg = [e2(nc.sbuf_tensor(f"kg{i}", [128, S], BF16)) for i in range(2)]
            vg = [e2(nc.sbuf_tensor(f"vg{i}", [128, S], BF16)) for i in range(2)]
            wqt = [e2(nc.sbuf_tensor(f"wqt{i}", [128, 2048], BF16)) for i in range(2)]
            qraw = e2(nc.sbuf_tensor("qraw", [128, 1024], F32))
            qswp = e2(nc.sbuf_tensor("qswp", [128, 1024], F32))
            qc1 = e2(nc.sbuf_tensor("qc1", [128, 1024], F32))
            qc2 = e2(nc.sbuf_tensor("qc2", [128, 1024], F32))
            qb = [e2(nc.sbuf_tensor(f"qb{i}", [128, 1024], BF16)) for i in range(2)]
            pwp = [e2(nc.sbuf_tensor(f"pwp{i}", [128, 1024], BF16)) for i in range(4)]
            ptw = [e2(nc.sbuf_tensor(f"ptw{i}", [128, 1024], BF16)) for i in range(3)]
            nmc = e2(nc.sbuf_tensor("nmc", [128, 32], F32))
            denc = e2(nc.sbuf_tensor("denc", [128, 32], F32))
            nmmin = e2(nc.sbuf_tensor("nmmin", [128, 2], F32))
            d8 = e2(nc.sbuf_tensor("d8", [128, 32], F32))
            u8 = e2(nc.sbuf_tensor("u8", [128, 32], F32))
            u8b = e2(nc.sbuf_tensor("u8b", [128, 32], F32))
            t8 = e2(nc.sbuf_tensor("t8", [128, 32], F32))
            denf = e2(nc.sbuf_tensor("denf", [128, 2], F32))
            rden = e2(nc.sbuf_tensor("rden", [128, 2], F32))
            accw = [e2(nc.sbuf_tensor(f"accw{i}", [128, 1024], F32)) for i in range(2)]
            accf = [e2(nc.sbuf_tensor(f"accf{i}", [128, 128], F32)) for i in range(2)]
            attt = [e2(nc.sbuf_tensor(f"attt{i}", [128, 128], BF16)) for i in range(2)]
            psqk = [e2(nc.psum_tensor(f"psqk{i}", [128, 1024], F32)) for i in range(2)]
            pstr = [e2(nc.psum_tensor(f"pstr{i}", [128, 1024], BF16)) for i in range(2)]
            pspv = [e2(nc.psum_tensor(f"pspv{i}", [128, 512], F32)) for i in range(2)]

            sd.emit("sync", lambda q: q.dma_start(out=cosq_s[:], in_=cosq), inc="dT", n=16)
            tok_qt = sd.emit("sync", lambda q: q.dma_start(out=sinq_s[:], in_=sinq), inc="dT", n=16)

            qkp = Pool(2)
            pwq = Pool(4)
            trp = Pool(2)
            ptwp = Pool(3)
            pvp = Pool(2)
            ptw_alt = [0]
            kg_free = [None, None]
            vg_free = [None, None]
            wqt_free = [None, None]
            last_qrope = [None]
            tok_attfm = None

            for g in range(NKV):
                gb = g % 2
                tkg = sd.emit("sync", (lambda g, gb: lambda q: q.dma_start(
                    out=kg[gb][:], in_=kdr[g, :, :]))(g, gb),
                    waits=[tok_spill_all, kg_free[gb]], inc="dK", n=16)
                tvg = sd.emit("sync", (lambda g, gb: lambda q: q.dma_start(
                    out=vg[gb][:], in_=vdr[g, :, :]))(g, gb),
                    waits=[vg_free[gb]], inc="dK", n=16)
                for pair2 in range(2):
                    heads = [4 * g + 2 * pair2, 4 * g + 2 * pair2 + 1]
                    tokqs = [None, None]
                    acc_toks_all = [[], []]
                    merge_tok = [None, None]
                    for ln_, h in enumerate(heads):
                        hb = h % 2
                        twq = sd.emit("gpsimd", (lambda hb, h: lambda gp: gp.dma_start(
                            out=wqt[hb][:], in_=wq[h, :, :]))(hb, h),
                            waits=[wqt_free[hb]], inc="dG", n=16)
                        sl, sfree = qkp.get()
                        for half in range(2):
                            for dc in range(16):
                                w0 = [twq, tok_xnown, sfree] if (half == 0 and dc == 0) else []
                                mm = sd.emit("tensor", (lambda sl, hb, half, dc: lambda t: t.matmul(
                                    psqk[sl][:, 512 * half:512 * (half + 1)],
                                    wqt[hb][:, 128 * dc:128 * (dc + 1)],
                                    xnown[:, 1024 * dc + 512 * half:1024 * dc + 512 * (half + 1)],
                                    start=(dc == 0), stop=(dc == 15)))(sl, hb, half, dc),
                                    waits=w0, inc="tt" if dc == 15 else None)
                        wqt_free[hb] = mm
                        tqe = sd.emit("vector", (lambda sl, h: lambda v: v.tensor_scalar_add(
                            out=qraw[:], in0=psqk[sl][:], scalar1=bq_s[:, h:h + 1]))(sl, h),
                            waits=[mm, last_qrope[0]], inc="vv")
                        qkp.put(sl, tqe)
                        tshf = sd.emit("vector", lambda v: v.stream_shuffle(
                            out=qswp[:], in_=qraw[:], mask=SWAP32), inc="vv")
                        sd.emit("gpsimd", lambda v: v.tensor_tensor(
                            out=qc1[:], in0=qswp[:], in1=sinq_s[:], op=OP.mult), waits=[tok_qt, tshf])
                        sd.emit("gpsimd", lambda v: v.tensor_tensor(
                            out=qc2[:], in0=qraw[:], in1=cosq_s[:], op=OP.mult))
                        tokqs[ln_] = sd.emit("gpsimd", (lambda ln_: lambda v: v.tensor_tensor(
                            out=qb[ln_][:], in0=qc1[:], in1=qc2[:], op=OP.add))(ln_), inc="gg")
                        last_qrope[0] = tokqs[ln_]

                    def tail_te(ln_, pws, te, p):
                        sl3, tfree = trp.get()
                        for i8 in range(8):
                            ttr = sd.emit("tensor", (lambda pws, sl3, i8: lambda t: t.transpose(
                                pstr[sl3][:, 128 * i8:128 * (i8 + 1)],
                                pwp[pws][:, 128 * i8:128 * (i8 + 1)], identb[:]))(pws, sl3, i8),
                                waits=[te, tfree] if i8 == 0 else [],
                                inc="tt" if i8 == 7 else None)
                        pwq.put(pws, ttr)
                        pts, ptfree = ptwp.get()
                        ptw_alt[0] ^= 1
                        if ptw_alt[0]:
                            tpt = sd.emit("scalar", (lambda sl3, pts: lambda s_: s_.copy(
                                out=ptw[pts][:], in_=pstr[sl3][:]))(sl3, pts), waits=[ttr, ptfree], inc="ss")
                        else:
                            tpt = sd.emit("vector", (lambda sl3, pts: lambda v: v.tensor_copy(
                                out=ptw[pts][:], in_=pstr[sl3][:]))(sl3, pts), waits=[ttr, ptfree], inc="vv")
                        trp.put(sl3, tpt)
                        return (pts, tpt, p)

                    def tail_pv(ln_, pts, tpt, p):
                        sl4, pfree4 = pvp.get()
                        for i8 in range(8):
                            tpv = sd.emit("tensor", (lambda pts, sl4, p, i8, gb: lambda t: t.matmul(
                                pspv[sl4][:, 0:128], ptw[pts][:, 128 * i8:128 * (i8 + 1)],
                                vg[gb][:, 1024 * p + 128 * i8:1024 * p + 128 * (i8 + 1)],
                                start=(i8 == 0), stop=(i8 == 7)))(pts, sl4, p, i8, gb),
                                waits=[tpt, tvg, pfree4] if i8 == 0 else [],
                                inc="tt" if i8 == 7 else None)
                        vg_free[gb] = tpv
                        ptwp.put(pts, tpv)
                        if p % 2 == 0:
                            ta = sd.emit("vector", (lambda ln_, sl4, p: lambda v: v.tensor_copy(
                                out=accw[ln_][:, 128 * p:128 * (p + 1)], in_=pspv[sl4][:, 0:128]))(ln_, sl4, p),
                                waits=[tpv, merge_tok[ln_]], inc="vv")
                        else:
                            ta = sd.emit("scalar", (lambda ln_, sl4, p: lambda s_: s_.copy(
                                out=accw[ln_][:, 128 * p:128 * (p + 1)], in_=pspv[sl4][:, 0:128]))(ln_, sl4, p),
                                waits=[tpv, merge_tok[ln_]], inc="ss")
                        merge_tok[ln_] = None
                        acc_toks_all[ln_].append(ta)
                        pvp.put(sl4, ta)

                    def do_merge(ln_, h, l, nP):
                        co = 16 * ln_
                        gv = sd.last("vv")
                        sd.emit("vector", (lambda ln_, co, nP: lambda v: v.tensor_reduce(
                            out=nmmin[:, ln_:ln_ + 1], in_=nmc[:, co:co + nP], axis=AX.X, op=OP.min))(ln_, co, nP),
                            waits=[gv], inc="vv")
                        g1 = sd.last("vv")
                        td8 = sd.emit("vector", (lambda ln_, co, nP: lambda v: v.tensor_single_scalar(
                            out=d8[:, co:co + nP], in_=nmc[:, co:co + nP],
                            scalar=nmmin[:, ln_:ln_ + 1], op=OP.subtract))(ln_, co, nP),
                            waits=[g1], inc="vv")
                        tu8 = sd.emit("scalar", (lambda co, nP: lambda s_: s_.activation(
                            out=u8[:, co:co + nP], in_=d8[:, co:co + nP], func=AF.Exp, scale=-1.0))(co, nP),
                            waits=[td8], inc="ss")
                        sd.emit("vector", (lambda co, nP: lambda v: v.tensor_tensor(
                            out=t8[:, co:co + nP], in0=u8[:, co:co + nP],
                            in1=denc[:, co:co + nP], op=OP.mult))(co, nP),
                            waits=[tu8], inc="vv")
                        g2 = sd.last("vv")
                        sd.emit("vector", (lambda ln_, co, nP: lambda v: v.reduce_sum(
                            out=denf[:, ln_:ln_ + 1], in_=t8[:, co:co + nP], axis=AX.X))(ln_, co, nP),
                            waits=[g2], inc="vv")
                        g3 = sd.last("vv")
                        sd.emit("vector", (lambda ln_: lambda v: v.reciprocal(
                            out=rden[:, ln_:ln_ + 1], in_=denf[:, ln_:ln_ + 1]))(ln_),
                            waits=[g3], inc="vv")
                        g4 = sd.last("vv")
                        sd.emit("vector", (lambda ln_, co, nP: lambda v: v.tensor_single_scalar(
                            out=u8b[:, co:co + nP], in_=u8[:, co:co + nP],
                            scalar=rden[:, ln_:ln_ + 1], op=OP.mult))(ln_, co, nP),
                            waits=[g4], inc="vv")
                        g5 = sd.last("vv")
                        acc_toks = [t for t in acc_toks_all[ln_] if t is not None]
                        vv_last = max((v for sm, v in acc_toks if sm == "vv"), default=None)
                        ss_last = max((v for sm, v in acc_toks if sm == "ss"), default=None)
                        accwait = [g5]
                        if vv_last is not None:
                            accwait.append(("vv", vv_last))
                        if ss_last is not None:
                            accwait.append(("ss", ss_last))
                        if nP == 1:
                            tat = sd.emit("vector", (lambda ln_, co: lambda v: v.tensor_single_scalar(
                                out=attt[ln_][:], in_=accw[ln_][:, 0:128],
                                scalar=u8b[:, co:co + 1], op=OP.mult))(ln_, co),
                                waits=accwait, inc="vv")
                        else:
                            sd.emit("vector", (lambda ln_, co: lambda v: v.tensor_single_scalar(
                                out=accf[ln_][:], in_=accw[ln_][:, 0:128],
                                scalar=u8b[:, co:co + 1], op=OP.mult))(ln_, co),
                                waits=accwait)
                            for p in range(1, nP - 1):
                                sd.emit("vector", (lambda ln_, co, p: lambda v: v.scalar_tensor_tensor(
                                    out=accf[ln_][:], in0=accw[ln_][:, 128 * p:128 * (p + 1)],
                                    scalar=u8b[:, co + p:co + p + 1], in1=accf[ln_][:],
                                    op0=OP.mult, op1=OP.add))(ln_, co, p))
                            tat = sd.emit("vector", (lambda ln_, co, nP: lambda v: v.scalar_tensor_tensor(
                                out=attt[ln_][:], in0=accw[ln_][:, 128 * (nP - 1):128 * nP],
                                scalar=u8b[:, co + nP - 1:co + nP], in1=accf[ln_][:],
                                op0=OP.mult, op1=OP.add))(ln_, co, nP), inc="vv")
                        acc_toks_all[ln_] = []
                        merge_tok[ln_] = tat
                        sl3, tfree = trp.get()
                        ttr = sd.emit("tensor", (lambda ln_, sl3: lambda t: t.transpose(
                            pstr[sl3][:, 0:128], attt[ln_][:], identb[:]))(ln_, sl3),
                            waits=[tat, tfree], inc="tt")
                        taf = sd.emit("scalar", (lambda sl3, h, l: lambda s_: s_.copy(
                            out=attfm[:, 1024 * h + 128 * l:1024 * h + 128 * (l + 1)],
                            in_=pstr[sl3][:, 0:128]))(sl3, h, l), waits=[ttr], inc="ss")
                        trp.put(sl3, taf)
                        return taf

                    def lane_gen(ln_, h):
                        nonlocal tok_attfm
                        tokq = tokqs[ln_]
                        for l in range(NLT):
                            nP = l + 1
                            pend = []
                            pend2 = []
                            for p in range(nP):
                                sl2, sfree2 = qkp.get()
                                sd.emit("tensor", (lambda sl2, ln_, l, p, gb: lambda t: t.matmul(
                                    psqk[sl2][:, 0:512], qb[ln_][:, 128 * l:128 * (l + 1)],
                                    kg[gb][:, 1024 * p:1024 * p + 512],
                                    start=True, stop=True))(sl2, ln_, l, p, gb),
                                    waits=[tokq, tkg, sfree2], inc=None)
                                tqk = sd.emit("tensor", (lambda sl2, ln_, l, p, gb: lambda t: t.matmul(
                                    psqk[sl2][:, 512:1024], qb[ln_][:, 128 * l:128 * (l + 1)],
                                    kg[gb][:, 1024 * p + 512:1024 * (p + 1)],
                                    start=True, stop=True))(sl2, ln_, l, p, gb), inc="tt")
                                kg_free[gb] = tqk
                                if p == nP - 1:
                                    sd.emit("vector", (lambda sl2: lambda v: v.tensor_tensor(
                                        out=psqk[sl2][:, 0:512], in0=psqk[sl2][:, 0:512],
                                        in1=mska_s[:], op=OP.add))(sl2), waits=[tqk, tok_consts])
                                    sd.emit("vector", (lambda sl2: lambda v: v.tensor_tensor(
                                        out=psqk[sl2][:, 512:1024], in0=psqk[sl2][:, 512:1024],
                                        in1=mskb_s[:], op=OP.add))(sl2))
                                    wred = []
                                else:
                                    wred = [tqk]
                                tm = sd.emit("vector", (lambda sl2, co, p: lambda v: v.reduce_max(
                                    nmc[:, co + p:co + p + 1], psqk[sl2][:], axis=AX.X,
                                    negate=True))(sl2, 16 * ln_, p),
                                    waits=wred, inc="vv")
                                pws, pwfree = pwq.get()
                                te = sd.emit("scalar", (lambda sl2, pws, co, p: lambda s_: s_.activation(
                                    out=pwp[pws][:], in_=psqk[sl2][:], func=AF.Exp,
                                    bias=nmc[:, co + p:co + p + 1], scale=1.0,
                                    accum_out=denc[:, co + p:co + p + 1]))(sl2, pws, 16 * ln_, p),
                                    waits=[tm, pwfree], inc="ss")
                                qkp.put(sl2, te)
                                pend.append((pws, te, p))
                                if len(pend) > 2:
                                    pend2.append(tail_te(ln_, *pend.pop(0)))
                                if len(pend2) > 1:
                                    tail_pv(ln_, *pend2.pop(0))
                                yield
                            while pend:
                                pend2.append(tail_te(ln_, *pend.pop(0)))
                            while pend2:
                                tail_pv(ln_, *pend2.pop(0))
                            tok_attfm = do_merge(ln_, h, l, nP)
                            yield

                    gens = [lane_gen(0, heads[0]), lane_gen(1, heads[1])]
                    alive = [True, True]
                    while alive[0] or alive[1]:
                        for gi in (0, 1):
                            if alive[gi]:
                                try:
                                    next(gens[gi])
                                except StopIteration:
                                    alive[gi] = False

        sd.barrier()
        with ExitStack() as s3:
            e3 = s3.enter_context
            outfm = e3(nc.sbuf_tensor("outfm", [128, 16 * 1024], F32))
            wob = [e3(nc.sbuf_tensor(f"wob{i}", [128, 2048], BF16)) for i in range(2)]
            ffnrt = [e3(nc.sbuf_tensor(f"ffnrt{i}", [128, 1024], F32)) for i in range(2)]
            psA = [e3(nc.psum_tensor(f"psA{i}", [128, 512], F32)) for i in range(2)]

            wob_free = [None, None]
            ffnrt_free = [None, None]
            psa = Pool(2)
            for oc in range(16):
                ob = oc % 2
                two = sd.emit("gpsimd", (lambda ob, oc: lambda gp: gp.dma_start(
                    out=wob[ob][:], in_=wo[oc, :, :]))(ob, oc),
                    waits=[wob_free[ob]], inc="dG", n=16)
                tfr = sd.emit("sync", (lambda ob, oc: lambda q: q.dma_start(
                    out=ffnrt[ob][:], in_=fdr[:, 1024 * oc:1024 * (oc + 1)]))(ob, oc),
                    waits=[tok_fsp, ffnrt_free[ob]], inc="dK", n=16)
                for half in range(2):
                    pj, pfree = psa.get()
                    for ec in range(16):
                        w0 = [two, tok_attfm, pfree] if ec == 0 else []
                        mm = sd.emit("tensor", (lambda pj, ob, ec, half: lambda t: t.matmul(
                            psA[pj][:], wob[ob][:, 128 * ec:128 * (ec + 1)],
                            attfm[:, 1024 * ec + 512 * half:1024 * ec + 512 * (half + 1)],
                            start=(ec == 0), stop=(ec == 15)))(pj, ob, ec, half),
                            waits=w0, inc="tt" if ec == 15 else None)
                    tev = sd.emit("vector", (lambda pj, ob, oc, half: lambda v: v.scalar_tensor_tensor(
                        out=outfm[:, 1024 * oc + 512 * half:1024 * oc + 512 * (half + 1)],
                        in0=psA[pj][:], scalar=bo_s[:, oc:oc + 1],
                        in1=ffnrt[ob][:, 512 * half:512 * (half + 1)],
                        op0=OP.add, op1=OP.add))(pj, ob, oc, half),
                        waits=[mm, tfr], inc="vv")
                    psa.put(pj, tev)
                    ffnrt_free[ob] = tev
                wob_free[ob] = mm
            tok_p2 = sd.last("vv")
            xot = [e3(nc.sbuf_tensor(f"xot{i}", [128, 2048], F32)) for i in range(2)]
            otm = [e3(nc.sbuf_tensor(f"otm{i}", [128, 2048], F32)) for i in range(2)]
            pse = [e3(nc.psum_tensor(f"pse{i}", [128, 512], F32)) for i in range(2)]
            xot_free = [None, None]
            otm_free = [None, None]
            pse_pool = Pool(2)
            for l in range(NLT):
                lb = l % 2
                txo = sd.emit("sync", (lambda l, lb: lambda q: q.dma_start(
                    out=xot[lb][:], in_=xown[l, :, :]))(l, lb),
                    waits=[xot_free[lb]], inc="dX", n=16)
                tadds = []
                for b4 in range(4):
                    pj, pfree = pse_pool.get()
                    for j in range(4):
                        oc = 4 * b4 + j
                        ttr = sd.emit("tensor", (lambda pj, oc, l, j: lambda t: t.transpose(
                            pse[pj][:, 128 * j:128 * (j + 1)],
                            outfm[:, 1024 * oc + 128 * l:1024 * oc + 128 * (l + 1)], identf[:]))(pj, oc, l, j),
                            waits=[tok_p2, pfree] if j == 0 else [],
                            inc="tt" if j == 3 else None)
                    tadd = sd.emit("vector", (lambda pj, lb, b4: lambda v: v.tensor_tensor(
                        out=otm[lb][:, 512 * b4:512 * (b4 + 1)], in0=pse[pj][:],
                        in1=xot[lb][:, 512 * b4:512 * (b4 + 1)], op=OP.add))(pj, lb, b4),
                        waits=[ttr, txo] + ([otm_free[lb]] if b4 == 0 else []), inc="vv")
                    pse_pool.put(pj, tadd)
                    tadds.append(tadd)
                xot_free[lb] = tadds[-1]
                tst = sd.emit("sync", (lambda l, lb: lambda q: q.dma_start(
                    out=oout[l, :, :], in_=otm[lb][:]))(l, lb),
                    waits=[tadds[-1]], inc="dO", n=16)
                otm_free[lb] = tst

            fin = sd.last("dO")
            sd.emit("sync", lambda q: q.nop(), waits=[fin])

            semnames = sorted(sd.cnt.keys()) + [s for s in ("dM",) if s not in sd.cnt]
            with ExitStack() as sems_stack:
                sems = {s: sems_stack.enter_context(nc.semaphore(s)) for s in set(semnames)}
                with nc.Block() as block:
                    @block.sync
                    def _(q):
                        sd.replay("sync", q, sems)

                    @block.vector
                    def _(v):
                        sd.replay("vector", v, sems)

                    @block.scalar
                    def _(s):
                        sd.replay("scalar", s, sems)

                    @block.tensor
                    def _(t):
                        sd.replay("tensor", t, sems)

                    @block.gpsimd
                    def _(gp):
                        sd.replay("gpsimd", gp, sems)
        sA.close()
    return nc


def _tables():
    inv = (np.float32(1.0) /
           (np.float32(10000.0) ** (np.arange(0, HD, 2, dtype=np.float32) / np.float32(HD))))
    freqs = np.arange(S, dtype=np.float32)[:, None] * inv[None, :].astype(np.float32)
    cos = np.cos(freqs.astype(np.float64)).astype(np.float32)
    sin = np.sin(freqs.astype(np.float64)).astype(np.float32)
    cos_fm = np.repeat(cos.T, 2, axis=0)
    sgn = np.where(np.arange(HD) % 2 == 0, -1.0, 1.0).astype(np.float32)[:, None]
    sin_fm = np.repeat(sin.T, 2, axis=0) * sgn
    return cos_fm.astype(np.float32), sin_fm.astype(np.float32)


_NC_CACHE = {}


def kernel(L0_attn_x, L0_attn_Wq, L0_attn_bq, L0_attn_Wk, L0_attn_bk,
           L0_attn_Wv, L0_attn_bv, L0_attn_Wo, L0_attn_bo,
           L0_ffn_up_weight, L0_ffn_down_weight, L0_ffn_up_scale,
           L0_ffn_down_scale, L0_ffn_output_scale):
    f32 = np.float32
    x = np.ascontiguousarray(np.asarray(L0_attn_x, f32).reshape(S, D))
    wq_b = np.ascontiguousarray(np.asarray(L0_attn_Wq, f32).reshape(NH, HD, 16, 128).transpose(0, 3, 2, 1))
    wk_t = np.ascontiguousarray(np.asarray(L0_attn_Wk, f32).reshape(512, 16, 128).transpose(2, 1, 0))
    wv_t = np.ascontiguousarray(np.asarray(L0_attn_Wv, f32).reshape(512, 16, 128).transpose(2, 1, 0))
    wo_b = np.ascontiguousarray(np.asarray(L0_attn_Wo, f32).reshape(16, 128, 16, 128).transpose(0, 3, 2, 1))
    wup_b = np.ascontiguousarray(np.asarray(L0_ffn_up_weight, f32).reshape(4, 16, 128, 16, 128).transpose(0, 1, 4, 3, 2))
    wd_b = np.ascontiguousarray(np.asarray(L0_ffn_down_weight, f32).reshape(16, 128, 4, 16, 128).transpose(2, 0, 4, 3, 1))
    bq_c = np.ascontiguousarray(np.asarray(L0_attn_bq, f32).reshape(16, 128).T)
    bk_c = np.ascontiguousarray(np.asarray(L0_attn_bk, f32).reshape(4, 128).T)
    bv_c = np.ascontiguousarray(np.asarray(L0_attn_bv, f32).reshape(4, 128).T)
    bo_c = np.ascontiguousarray(np.asarray(L0_attn_bo, f32).reshape(16, 128).T)
    up_c = np.ascontiguousarray(np.asarray(L0_ffn_up_scale, f32).reshape(64, 128).T)
    ds_c = np.ascontiguousarray(np.asarray(L0_ffn_down_scale, f32).reshape(16, 128).T)
    os_r = np.ascontiguousarray(np.broadcast_to(np.asarray(L0_ffn_output_scale, f32).reshape(1, 1), (128, 16)))
    cos_fm, sin_fm = _tables()
    identb = np.eye(128, dtype=ml_dtypes.bfloat16)
    identf = np.eye(128, dtype=f32)

    if "nc" not in _NC_CACHE:
        _NC_CACHE["nc"] = build_nc()
    nc = _NC_CACHE["nc"]

    jcol = np.arange(512)[None, :]
    prow = np.arange(128)[:, None]
    in_maps = []
    own_rows_all = []
    for c in range(NC):
        own = np.concatenate([(8 * l + c) * 128 + np.arange(128) for l in range(NLT)])
        own_rows_all.append(own)
        mska = np.where(jcol <= 128 * c + prow, 0.0, NEGBIG).astype(f32)
        mskb = np.where(jcol <= 128 * c + prow - 512, 0.0, NEGBIG).astype(f32)
        in_maps.append(dict(
            xfull=x, xown=np.ascontiguousarray(x[own].reshape(NLT, 128, D)),
            wq=wq_b, wk=wk_t, wv=wv_t, wo=wo_b, wup=wup_b, wd=wd_b,
            bqc=bq_c, bkc=bk_c, bvc=bv_c, boc=bo_c,
            upsc=up_c, dsc=ds_c, osc=os_r,
            cosk=cos_fm, sink=sin_fm,
            cosq=np.ascontiguousarray((ALPHA * cos_fm)[:, own]),
            sinq=np.ascontiguousarray((ALPHA * sin_fm)[:, own]),
            mska=mska, mskb=mskb, idb=identb, idf=identf,
        ))
    res = run_bass_kernel_spmd(nc, in_maps, core_ids=list(range(NC)))
    out = np.empty((S, D), f32)
    for c in range(NC):
        out[own_rows_all[c]] = np.asarray(res.results[c]["oout"], f32).reshape(NLT * 128, D)
    return out.reshape(1, S, D)
